# version 34
# baseline (speedup 1.0000x reference)
import contextlib
import numpy as np
import concourse.bass as bass
import concourse.mybir as mybir
from concourse.bass_utils import run_bass_kernel_spmd

F32 = mybir.dt.float32
BF16 = mybir.dt.bfloat16
AF = mybir.ActivationFunctionType
ALU = mybir.AluOpType
AX = mybir.AxisListType

D = 1024
T = 2048
NH = 16
DH = 64
NJ = 8
NTL = 16
NBLK = 4
BLK = 512
EPS = 1e-5
PAIRS = [[0, 1], [2, 3], [4, 5], [6, 7]]

ENGINES = ("sync", "vector", "scalar", "tensor", "gpsimd")
N_DMA_LANES = 8


class _Op:
    __slots__ = ("idx", "eng", "fn", "deps", "kind", "sem", "val", "needs_inc", "tag")

    def __init__(self, idx, eng, fn, deps, kind, tag):
        self.idx = idx
        self.eng = eng
        self.fn = fn
        self.deps = deps
        self.kind = kind
        self.sem = None
        self.val = None
        self.needs_inc = False
        self.tag = tag


class Prog:
    def __init__(self, nc):
        self.nc = nc
        self.ops = []
        self.last_w = {}
        self.readers = {}

    def add(self, eng, fn, reads=(), writes=(), kind="compute", tag=""):
        deps = set()
        for k in reads:
            i = self.last_w.get(k)
            if i is not None:
                deps.add(i)
        for k in writes:
            i = self.last_w.get(k)
            if i is not None:
                deps.add(i)
            for r in self.readers.get(k, ()):
                deps.add(r)
        idx = len(self.ops)
        self.ops.append(_Op(idx, eng, fn, deps, kind, tag))
        for k in reads:
            self.readers.setdefault(k, []).append(idx)
        for k in writes:
            self.last_w[k] = idx
            self.readers[k] = []
        return idx

    def set_last_writer(self, key, idx):
        self.last_w[key] = idx
        self.readers[key] = []

    def emit(self, ex):
        nc = self.nc
        ops = self.ops
        for op in ops:
            tdeps = [d for d in op.deps if ops[d].eng == "tensor"]
            if len(tdeps) > 1:
                keep = max(tdeps)
                for d in tdeps:
                    if d != keep:
                        op.deps.discard(d)
        for op in ops:
            for d in op.deps:
                ops[d].needs_inc = True
        for op in ops:
            if op.kind in ("dma", "cc"):
                op.needs_inc = True

        tick = {}
        dma_lanes = {}
        any_compute = {e: any(o.eng == e and o.kind == "compute" and o.needs_inc
                              for o in ops) for e in ENGINES}
        n_dma = {e: sum(1 for o in ops if o.eng == e and o.kind == "dma")
                 for e in ENGINES}
        for e in ENGINES:
            if any_compute[e]:
                tick[e] = ex.enter_context(nc.semaphore(f"tick_{e}"))
            dma_lanes[e] = [ex.enter_context(nc.semaphore(f"dl_{e}{i}"))
                            for i in range(min(N_DMA_LANES, n_dma[e]))]
        n_cc = sum(1 for o in ops if o.kind == "cc")
        cc_sems = [ex.enter_context(nc.semaphore(f"cc{i}")) for i in range(n_cc)]

        cnt = {}
        dma_i = {e: 0 for e in ENGINES}
        cc_cnt = 0
        lane_prev = {}
        for op in ops:
            if op.kind == "compute":
                if op.needs_inc:
                    s = tick[op.eng]
                    cnt[s.num] = cnt.get(s.num, 0) + 1
                    op.sem, op.val = s, cnt[s.num]
            elif op.kind == "dma":
                lanes = dma_lanes[op.eng]
                s = lanes[dma_i[op.eng] % len(lanes)]
                dma_i[op.eng] += 1
                cnt[s.num] = cnt.get(s.num, 0) + 16
                op.sem, op.val = s, cnt[s.num]
                if s.num in lane_prev:
                    op.deps.add(lane_prev[s.num])
                lane_prev[s.num] = op.idx
            elif op.kind == "cc":
                op.sem, op.val = cc_sems[cc_cnt], 1
                cc_cnt += 1

        snapshots = {}
        know = {e: {} for e in ENGINES}
        waits_for_op = [None] * len(ops)
        for op in ops:
            k = know[op.eng]
            w = []
            for d in sorted(op.deps):
                dop = ops[d]
                if dop.sem is None:
                    raise RuntimeError(f"dep of {op.tag or op.idx} on "
                                       f"{dop.tag or d} has no sem")
                if k.get(dop.sem.num, 0) >= dop.val:
                    continue
                w.append((dop.sem, dop.val))
                k[dop.sem.num] = dop.val
                snap = snapshots.get((dop.sem.num, dop.val))
                if snap:
                    for s2, v2 in snap.items():
                        if k.get(s2, 0) < v2:
                            k[s2] = v2
            waits_for_op[op.idx] = w
            if op.needs_inc:
                snap = dict(k)
                snap[op.sem.num] = op.val
                snapshots[(op.sem.num, op.val)] = snap

        by_eng = {e: [o for o in ops if o.eng == e] for e in ENGINES}
        final = dict(cnt)
        for s in cc_sems:
            final[s.num] = 1
        sem_by_id = {}
        for e in ENGINES:
            if e in tick:
                sem_by_id[tick[e].num] = tick[e]
            for s in dma_lanes[e]:
                sem_by_id[s.num] = s
        for s in cc_sems:
            sem_by_id[s.num] = s

        block = ex.enter_context(nc.Block())

        def run_engine(engobj, eng):
            for op in by_eng[eng]:
                for sem, val in waits_for_op[op.idx]:
                    engobj.wait_ge(sem, val)
                inst = op.fn(engobj)
                if op.needs_inc:
                    if op.kind == "dma":
                        inst.then_inc(op.sem, 16)
                    elif op.kind == "cc":
                        inst.then_inc(op.sem)
                    else:
                        inst.then_inc(op.sem, 1)

        @block.sync
        def _(eng):
            run_engine(eng, "sync")
            for sid, v in sorted(final.items()):
                eng.wait_ge(sem_by_id[sid], v)

        @block.vector
        def _(eng):
            run_engine(eng, "vector")

        @block.scalar
        def _(eng):
            run_engine(eng, "scalar")

        @block.tensor
        def _(eng):
            run_engine(eng, "tensor")

        @block.gpsimd
        def _(eng):
            run_engine(eng, "gpsimd")


def build_nc():
    nc = bass.Bass("TRN2", target_bir_lowering=False)

    x_in = nc.dram_tensor("x", [T, D], F32, kind="ExternalInput").ap()
    xbnd = nc.dram_tensor("xbnd", [1, D], F32, kind="ExternalInput").ap()
    oddm_in = nc.dram_tensor("oddmask", [1, 1], F32, kind="ExternalInput").ap()
    idf = nc.dram_tensor("idf", [128, 128], F32, kind="ExternalInput").ap()
    ct = {w: nc.dram_tensor(f"ct_{w}", [D, D], F32, kind="ExternalInput").ap()
          for w in ("q", "k", "v", "o")}
    bvec = {w: nc.dram_tensor(f"b_{w}", [128, NJ], F32, kind="ExternalInput").ap()
            for w in ("q", "k", "v", "o")}
    mu_in = {w: nc.dram_tensor(f"mu_{w}", [128, NJ], F32, kind="ExternalInput").ap()
             for w in ("r", "k")}
    wT = {w: nc.dram_tensor(f"w{w}T", [D, D], F32, kind="ExternalInput").ap()
          for w in ("r", "k", "v")}
    y_out = nc.dram_tensor("y", [T, D], F32, kind="ExternalOutput").ap()

    kC = nc.dram_tensor("kC", [D, T], BF16).ap()
    kG = nc.dram_tensor("kG", [2 * D, T], BF16).ap()
    vC = nc.dram_tensor("vC", [NH * 128, 1040], BF16).ap()
    vG = nc.dram_tensor("vG", [2 * NH * 128, 1040], BF16).ap()
    bC = nc.dram_tensor("bC", [1, D], F32).ap()
    mxC = nc.dram_tensor("mxC", [2, 16], F32).ap()
    mxR = nc.dram_tensor("mxR", [2, 16], F32).ap()
    bG = nc.dram_tensor("bG", [2, D], F32).ap()
    wbf = nc.dram_tensor("wbf", [128, 24576], BF16).ap()

    ex = contextlib.ExitStack()
    with ex:
        def S(name, shape, dt=F32):
            return ex.enter_context(nc.sbuf_tensor(name, shape, dt))
        def P(name, shape, dt=F32):
            return ex.enter_context(nc.psum_tensor(name, shape, dt))

        resid = [S(f"res{i}", [128, D]) for i in range(NTL)]
        arW = S("arW", [128, 24576], BF16)
        arS = S("arS", [128, 20480], BF16)
        arQ = S("arQ", [128, 16384], BF16)
        arX = S("arX", [128, 4160], BF16)

        bn12 = [S(f"bn12_{i}", [128, 12]) for i in range(2)]
        stb = S("stb", [128, 128])
        bnb12 = S("bnb12", [1, 12])
        sttb = S("sttb", [1, 32])

        muT = {w: S(f"muT_{w}", [128, NJ]) for w in ("r", "k")}
        bT = {w: S(f"bT_{w}", [128, NJ]) for w in ("q", "k", "v", "o")}
        idb = S("idb", [128, 128], BF16)
        onesb1 = S("onesb1", [1, 1], BF16)
        negonesf = S("negonesf", [1, 128])
        onesf = S("onesf", [1, 128])
        maskq = S("maskq", [128, 2], BF16)
        pairmask = S("pairmask", [2, 128])
        mrow = S("mrow", [1, 24])
        mstats = S("mstats", [2, 128])
        sqsc = S("sqsc", [128, 512], BF16)
        mne = S("mne", [128, 8])
        oddm = S("oddm", [1, 1])
        halfc = S("halfc", [128, 1])
        xtm = [S(f"xtm{i}", [128, D], BF16) for i in range(2)]

        arWf = arW[:, :].bitcast(F32)
        arSf = arS[:, :].bitcast(F32)
        arQf = arQ[:, :].bitcast(F32)
        resb = arSf[32:33, 7168:8192]
        xlnb = arS[0:1, 14336:15360]
        convst = [arQf[:, 1024 * i:1024 * (i + 1)] for i in range(2)]

        pA = P("pA", [128, 1024])
        pB = P("pB", [128, 1024])
        pC = P("pC", [128, 1024])
        pD = P("pD", [128, 1024])
        pDb = pD[:, :].bitcast(BF16)

        pr = Prog(nc)
        A = pr.add

        def wslot(i, n=1):
            return arW[:, 1024 * i:1024 * (i + 1 + (n - 1))]
        WK = [f"W{i}" for i in range(24)]
        SK = [f"S{i}" for i in range(20)]

        def wkeys(c0, c1):
            return [WK[i] for i in range(c0 // 1024, (c1 + 1023) // 1024)]
        def skeys(c0, c1):
            return [SK[i] for i in range(c0 // 1024, (c1 + 1023) // 1024)]

        for i in range(4):
            A("sync", lambda e, i=i: e.dma_start(out=resid[i][:, :], in_=x_in[128 * i:128 * (i + 1), :]),
              writes=[f"res{i}"], kind="dma")
        A("sync", lambda e: e.dma_start(out=resb, in_=xbnd), writes=["resb"], kind="dma")

        A("sync", lambda e: e.dma_start(out=convst[0][:, 0:128], in_=idf), writes=["Q0"], kind="dma")
        A("vector", lambda v: v.tensor_copy(idb[:, :], convst[0][:, 0:128]), reads=["Q0"], writes=["idb"])
        A("vector", lambda v: v.memset(onesb1[:, :], 1.0), writes=["onesb1"])
        A("vector", lambda v: v.memset(halfc[:, :], 0.5), writes=["halfc"])
        A("vector", lambda v: v.memset(onesf[:, :], 1.0), writes=["onesf"])
        A("vector", lambda v: v.memset(negonesf[:, :], -1.0), writes=["negonesf"])
        A("vector", lambda v: v.memset(maskq[:, :], 0.0), writes=["maskq"])
        A("vector", lambda v: v.memset(maskq[0:64, 0:1], 1.0), writes=["maskq"])
        A("vector", lambda v: v.memset(maskq[64:128, 1:2], 1.0), writes=["maskq"])
        A("vector", lambda v: v.memset(pairmask[:, :], 0.0), writes=["pairmask"])
        A("vector", lambda v: v.memset(pairmask[0:1, 0:64], 1.0), writes=["pairmask"])
        A("gpsimd", lambda g: g.dma_start(out=pairmask[1:2, 64:128], in_=onesf[0:1, 0:64]),
          reads=["onesf", "pairmask"], writes=["pairmask"], kind="dma")
        for w in ("r", "k"):
            A("sync", lambda e, w=w: e.dma_start(out=muT[w][:, :], in_=mu_in[w]), writes=[f"mu{w}"], kind="dma")
        for w in ("q", "k", "v", "o"):
            A("sync", lambda e, w=w: e.dma_start(out=bT[w][:, :], in_=bvec[w]),
              writes=[f"b{w}"], kind="dma")
        A("sync", lambda e: e.dma_start(out=oddm[:, :], in_=oddm_in), writes=["oddm"], kind="dma")

        conv_n = [0]

        def load_weight_chunks(dram_ap, slot0, tag):
            for j in range(NJ):
                ci = conv_n[0] % 2
                conv_n[0] += 1
                A("sync", lambda e, j=j, ci=ci, d=dram_ap: e.dma_start(
                    out=convst[ci], in_=d[128 * j:128 * (j + 1), :]),
                  writes=[f"Q{ci}"], kind="dma", tag=f"{tag}ld{j}")
                A("scalar", lambda v, j=j, ci=ci, s=slot0: v.copy(
                    wslot(s + j)[:, :], convst[ci]),
                  reads=[f"Q{ci}"], writes=[WK[slot0 + j]], kind="compute", tag=f"{tag}cv{j}")

        INT32 = mybir.dt.int32

        def ln_stats(ti, parity, tag):
            rk = f"res{ti}"
            b = bn12[parity]
            bk = f"bn{parity}"
            A("vector", lambda v: v.bn_stats(b[:, 0:6], resid[ti][:, 0:512]),
              reads=[rk], writes=[bk], tag=f"{tag}bnA")
            A("vector", lambda v: v.bn_stats(b[:, 6:12], resid[ti][:, 512:1024]),
              reads=[rk, bk], writes=[bk], tag=f"{tag}bnB")
            A("vector", lambda v, ti=ti: v.bn_aggr(stb[:, 8 * ti:8 * ti + 2], b[:, :]),
              reads=[bk], writes=[f"stbT{ti}"], tag=f"{tag}ag")

        def phase_stats(tag, tiles=True):
            if tiles:
                for ti in range(NTL):
                    ln_stats(ti, ti % 2, f"{tag}st{ti}")
            s3 = stb[:, :].rearrange("p (t c) -> p t c", c=8)
            def col(b, c):
                return s3[:, 4 * b:4 * (b + 1), c:c + 1].squeeze(2)
            for b in range(4):
                tk = [f"stbT{4 * b + u}" for u in range(4)]
                fk = f"stbF{b}"
                A("vector", lambda v, b=b: v.tensor_scalar(col(b, 2), col(b, 1), EPS, None, op0=ALU.add),
                  reads=tk, writes=[fk], tag=f"{tag}ve{b}")
                A("scalar", lambda s, b=b: s.activation(col(b, 3), col(b, 2), AF.Ln),
                  reads=[fk], writes=[fk], tag=f"{tag}ln{b}")
                A("scalar", lambda s, b=b: s.activation(col(b, 3), col(b, 3), AF.Exp, scale=-0.5),
                  reads=[fk], writes=[fk], tag=f"{tag}ex{b}")

        def ln_bnd(tag):
            A("vector", lambda v: v.bn_stats(bnb12[:, 0:6], resb[0:1, 0:512]),
              reads=["resb"], writes=["bnb"], tag=f"{tag}bnA")
            A("vector", lambda v: v.bn_stats(bnb12[:, 6:12], resb[0:1, 512:1024]),
              reads=["resb", "bnb"], writes=["bnb"], tag=f"{tag}bnB")
            A("vector", lambda v: v.bn_aggr(sttb[:, 0:2], bnb12[:, :]),
              reads=["bnb"], writes=["sttb"], tag=f"{tag}ag")
            A("vector", lambda v: v.tensor_scalar(sttb[:, 2:3], sttb[:, 1:2], EPS, None, op0=ALU.add),
              reads=["sttb"], writes=["sttb"], tag=f"{tag}ve")
            A("scalar", lambda s: s.activation(sttb[:, 3:4], sttb[:, 2:3], AF.Ln),
              reads=["sttb"], writes=["sttb"], tag=f"{tag}l")
            A("scalar", lambda s: s.activation(sttb[:, 3:4], sttb[:, 3:4], AF.Exp, scale=-0.5),
              reads=["sttb"], writes=["sttb"], tag=f"{tag}e")
            A("vector", lambda v: v.tensor_scalar(xlnb, resb,
                                                  sttb[:, 0:1], sttb[:, 3:4],
                                                  op0=ALU.subtract, op1=ALU.mult),
              reads=["resb", "sttb"], writes=[SK[14]], tag=f"{tag}ap")
            for j in range(NJ):
                i0 = A("tensor", lambda t, j=j: t.matmul(pD[:, j:j + 1],
                                                         xlnb[0:1, 128 * j:128 * (j + 1)],
                                                         onesb1[0:1, 0:1],
                                                         start=True, stop=True),
                       reads=[SK[14], "onesb1"], writes=[] if j else ["pD0"], tag=f"{tag}mm{j}")
            pr.set_last_writer("pD0", i0)
            A("vector", lambda v: v.tensor_copy(
                arX[:, 0:4160].rearrange("p (j c) -> p j c", c=520)[:, :, 0:1],
                pD[:, 0:8].rearrange("p (j one) -> p j one", one=1)),
              reads=["pD0"], writes=[f"X{j}" for j in range(NJ)], tag=f"{tag}ev")

        def xb_apply(blk, with_bnd, tag, xv, xkeys):
            if blk > 0 and with_bnd:
                A("vector", lambda v: v.tensor_copy(
                    xv.rearrange("p (j c) -> p j c", c=520)[:, :, 0:1],
                    xv.rearrange("p (j c) -> p j c", c=520)[:, :, 512:513]),
                  reads=xkeys, writes=xkeys, tag=f"{tag}bc")
            for u in range(4):
                ti = 4 * blk + u
                par = u % 2
                A("vector", lambda v, ti=ti, par=par: v.tensor_scalar(
                    xtm[par][:, :], resid[ti][:, :], stb[:, 8 * ti:8 * ti + 1],
                    stb[:, 8 * ti + 3:8 * ti + 4], op0=ALU.subtract, op1=ALU.mult),
                  reads=[f"res{ti}", f"stbF{blk}"], writes=[f"xtm{par}"], tag=f"{tag}ap{u}")
                pb = 1024 * (u % 2)
                pbk = f"pD{u % 2}"
                i0 = None
                for j in range(NJ):
                    ii = A("tensor", lambda t, j=j, par=par, pb=pb: t.transpose(
                        pDb[:, pb + 128 * j:pb + 128 * (j + 1)], xtm[par][:, 128 * j:128 * (j + 1)], idb[:, :]),
                        reads=[f"xtm{par}", "idb"], writes=([pbk] if j == 0 else []), tag=f"{tag}tr{u}_{j}")
                    i0 = ii
                pr.set_last_writer(pbk, i0)
                A("scalar", lambda v, u=u, pb=pb, xv=xv: v.copy(
                    xv.rearrange("p (j c) -> p j c", c=520)[:, :, 1 + 128 * u:1 + 128 * (u + 1)],
                    pDb[:, pb:pb + 1024].rearrange("p (j c) -> p j c", c=128)),
                  reads=[pbk], writes=xkeys, tag=f"{tag}te{u}")



        psum_pool = [pA, pB, pC]
        psum_keys = ["pA", "pB", "pC"]

        def proj_mm(wslot0, rhs_views, rhs_keys, m, pi, tag, w_views=None, w_keys=None):
            ps = psum_pool[pi]
            pk = psum_keys[pi]
            last = None
            for j in range(NJ):
                wv = w_views[j] if w_views is not None else wslot(wslot0 + j)
                wk = w_keys[j] if w_keys is not None else [WK[wslot0 + j]]
                last = A("tensor", lambda t, j=j, m=m, ps=ps, wv=wv: t.matmul(
                    ps[:, 0:512], wv[:, 128 * m:128 * (m + 1)],
                    rhs_views[j], start=(j == 0), stop=(j == NJ - 1)),
                    reads=wk + [rhs_keys[j]],
                    writes=([pk] if j == 0 else []), tag=f"{tag}m{j}")
            pr.set_last_writer(pk, last)
            return ps, pk

        def ffn_phase(tag, pre_stats=False, from_cache=False):
            if tag == "f1":
                for i in range(4, NTL):
                    A("sync", lambda e, i=i: e.dma_start(
                        out=resid[i][:, :], in_=x_in[128 * i:128 * (i + 1), :]),
                      writes=[f"res{i}"], kind="dma")
            if from_cache:
                for i in range(24):
                    A("sync", lambda e, i=i: e.dma_start(
                        out=wslot(i)[:, :], in_=wbf[:, 1024 * i:1024 * (i + 1)]),
                      reads=["wbf"], writes=[WK[i]], kind="dma", tag=f"{tag}wc{i}")
            else:
                load_weight_chunks(wT["r"], 0, f"{tag}wr")
                load_weight_chunks(wT["k"], 8, f"{tag}wk")
                load_weight_chunks(wT["v"], 16, f"{tag}wv")
            ln_bnd(f"{tag}lb")

            def bufs(blk):
                if blk % 2 == 0:
                    xr_v = [arS[:, 4096 + 512 * j:4096 + 512 * (j + 1)] for j in range(NJ)]
                    xr_k = [SK[(4096 + 512 * j) // 1024] for j in range(NJ)]
                    xk_v = [arS[:, 8192 + 512 * j:8192 + 512 * (j + 1)] for j in range(NJ)]
                    xk_k = [SK[(8192 + 512 * j) // 1024] for j in range(NJ)]
                else:
                    xr_v = [arQ[:, 4096 + 512 * j:4096 + 512 * (j + 1)] for j in range(NJ)]
                    xr_k = [f"Q{2 + (512 * j) // 2048}" for j in range(NJ)]
                    xk_v = [arQ[:, 8192 + 512 * j:8192 + 512 * (j + 1)] for j in range(NJ)]
                    xk_k = [f"Q{4 + (512 * j) // 2048}" for j in range(NJ)]
                return xr_v, xr_k, xk_v, xk_k

            out_v = [arQ[:, 12288 + 512 * j:12288 + 512 * (j + 1)] for j in range(NJ)]
            out_k = [f"Q{6 + (512 * j) // 2048}" for j in range(NJ)]

            def front_apply(blk):
                xb_apply(blk, True, f"{tag}b{blk}", arX[:, 0:4160],
                         [f"X{j}" for j in range(NJ)])
                A("gpsimd", lambda v: v.tensor_tensor(
                    arS[:, 0:4096].rearrange("p (j c) -> p j c", c=512),
                    arX[:, 0:4160].rearrange("p (j c) -> p j c", c=520)[:, :, 1:513],
                    arX[:, 0:4160].rearrange("p (j c) -> p j c", c=520)[:, :, 0:512],
                    op=ALU.subtract),
                  reads=[f"X{j}" for j in range(NJ)], writes=[SK[0], SK[1], SK[2], SK[3]],
                  tag=f"{tag}df{blk}")

            def front_mix(blk):
                xr_v, xr_k, xk_v, xk_k = bufs(blk)
                for j in range(NJ):
                    dv = arS[:, 512 * j:512 * (j + 1)]
                    dk = SK[(512 * j) // 1024]
                    A("vector", lambda v, j=j, dv=dv, xr=xr_v[j]: v.scalar_tensor_tensor(
                        xr, dv, muT["r"][:, j:j + 1], arX[:, 520 * j:520 * j + 512],
                        op0=ALU.mult, op1=ALU.add),
                      reads=[dk, "mur", f"X{j}"], writes=[xr_k[j]], tag=f"{tag}xr{blk}_{j}")
                    A("vector", lambda v, j=j, dv=dv, xk=xk_v[j]: v.scalar_tensor_tensor(
                        xk, dv, muT["k"][:, j:j + 1], arX[:, 520 * j:520 * j + 512],
                        op0=ALU.mult, op1=ALU.add),
                      reads=[dk, "muk", f"X{j}"], writes=[xk_k[j]], tag=f"{tag}xk{blk}_{j}")

            def back_r(blk):
                xr_v, xr_k, xk_v, xk_k = bufs(blk)
                for m in range(NJ):
                    ps, pk = proj_mm(0, xr_v, xr_k, m, m % 3, f"{tag}r{blk}_{m}")
                    A("scalar", lambda s, m=m, ps=ps: s.activation(
                        arS[:, 12288 + 512 * m:12288 + 512 * (m + 1)], ps[:, 0:512],
                        AF.Tanh, scale=0.5),
                      reads=[pk], writes=[SK[(12288 + 512 * m) // 1024]], tag=f"{tag}sg{blk}_{m}")
                    A("scalar", lambda s, m=m: s.activation(
                        arS[:, 12288 + 512 * m:12288 + 512 * (m + 1)],
                        arS[:, 12288 + 512 * m:12288 + 512 * (m + 1)],
                        AF.Identity, scale=0.5, bias=halfc[:, 0:1]),
                      reads=[SK[(12288 + 512 * m) // 1024], "halfc"],
                      writes=[SK[(12288 + 512 * m) // 1024]], tag=f"{tag}sa{blk}_{m}")

            def back_k(blk):
                xr_v, xr_k, xk_v, xk_k = bufs(blk)
                for m in range(NJ):
                    ps, pk = proj_mm(8, xk_v, xk_k, m, m % 3, f"{tag}k{blk}_{m}")
                    kv = arS[:, 16384 + 512 * m:16384 + 512 * (m + 1)]
                    kk = SK[(16384 + 512 * m) // 1024]
                    A("scalar", lambda s, ps=ps, kv=kv: s.activation(kv, ps[:, 0:512], AF.Relu),
                      reads=[pk], writes=[kk], tag=f"{tag}rl{blk}_{m}")
                    A("gpsimd", lambda v, kv=kv: v.tensor_tensor(kv, kv, kv, op=ALU.mult),
                      reads=[kk], writes=[kk], tag=f"{tag}sq{blk}_{m}")

            def back_v(blk):
                ka_v = [arS[:, 16384 + 512 * j:16384 + 512 * (j + 1)] for j in range(NJ)]
                ka_k = [SK[(16384 + 512 * j) // 1024] for j in range(NJ)]
                for m in range(NJ):
                    ps, pk = proj_mm(16, ka_v, ka_k, m, m % 3, f"{tag}v{blk}_{m}")
                    A("vector", lambda v, m=m, ps=ps: v.scalar_tensor_tensor(
                        out_v[m], ps[:, 0:512], halfc[:, 0:1],
                        arS[:, 12288 + 512 * m:12288 + 512 * (m + 1)],
                        op0=ALU.mult, op1=ALU.mult),
                      reads=[pk, SK[(12288 + 512 * m) // 1024], "halfc"],
                      writes=[out_k[m]], tag=f"{tag}o{blk}_{m}")
                for u in range(4):
                    ti = 4 * blk + u
                    pb = 1024 * (u % 2)
                    pbk = f"pD{u % 2}"
                    i0 = None
                    for m in range(NJ):
                        i0 = A("tensor", lambda t, m=m, u=u, pb=pb: t.transpose(
                            pDb[:, pb + 128 * m:pb + 128 * (m + 1)],
                            out_v[m][:, 128 * u:128 * (u + 1)], idb[:, :]),
                            reads=[out_k[m], "idb"],
                            writes=([pbk] if m == 0 else []), tag=f"{tag}ot{blk}_{u}_{m}")
                    pr.set_last_writer(pbk, i0)
                    A("vector", lambda v, ti=ti, pb=pb: v.tensor_tensor(
                        resid[ti][:, :], resid[ti][:, :], pDb[:, pb:pb + 1024], op=ALU.add),
                      reads=[pbk, f"res{ti}"], writes=[f"res{ti}"], tag=f"{tag}ra{blk}_{u}")
                    if tag == "f1":
                        ln_stats(ti, u % 2, f"qpre{ti}")

            phase_stats(f"{tag}ps", tiles=not pre_stats)
            front_apply(0)
            front_mix(0)
            for blk in range(NBLK):
                back_r(blk)
                if blk + 1 < NBLK:
                    front_apply(blk + 1)
                back_k(blk)
                if blk + 1 < NBLK:
                    front_mix(blk + 1)
                back_v(blk)

        ffn_phase("f1")

        for i in range(24):
            A("sync", lambda e, i=i: e.dma_start(
                out=wbf[:, 1024 * i:1024 * (i + 1)], in_=wslot(i)[:, :]),
              reads=[WK[i]], writes=["wbf"], kind="dma", tag=f"wsv{i}")

        for wi, w in enumerate(("q", "k", "v")):
            load_weight_chunks(ct[w], 8 * wi, f"qkv_{w}")
        vt_bufs = [(arS[:, 8192:9232], [SK[8], SK[9]]), (arS[:, 14400:15440], [SK[14], SK[15]])]
        A("vector", lambda v: v.memset(arS[:, 8192:9232], 1.0),
          writes=[SK[8], SK[9]], tag="vtones0")
        A("vector", lambda v: v.memset(arS[:, 14400:15440], 1.0),
          writes=[SK[14], SK[15]], tag="vtones1")
        A("vector", lambda v: v.memset(mstats[:, 0:32], 0.0), writes=["mstats"], tag="macc0")

        def qkv_xv(blk):
            if blk % 2 == 0:
                return arX[:, 0:4160], [f"X{j}" for j in range(NJ)]
            return arS[:, 10240:14400], [SK[10 + i] for i in range(5)]

        def qkv_q(blk):
            xv, xkeys = qkv_xv(blk)
            xln_views = [xv[:, 520 * j + 1:520 * j + 513] for j in range(NJ)]
            xk2 = [xkeys[(520 * j) // 1024] for j in range(NJ)] if len(xkeys) == 5 else xkeys
            for m in range(NJ):
                ps, pk = proj_mm(0, xln_views, xk2, m, m % 3, f"qq{blk}_{m}")
                A("scalar", lambda s, m=m, blk=blk, ps=ps: s.activation(
                    arQ[:, 2048 * m + 512 * blk:2048 * m + 512 * (blk + 1)], ps[:, 0:512],
                    AF.Identity, bias=bT["q"][:, m:m + 1]),
                  reads=[pk, "bq"], writes=[f"Q{m}"], tag=f"qe{blk}_{m}")
                A("scalar", lambda s, m=m, blk=blk: s.activation(
                    sqsc[:, :], arQ[:, 2048 * m + 512 * blk:2048 * m + 512 * (blk + 1)],
                    AF.Square),
                  reads=[f"Q{m}"], writes=["sqsc"], tag=f"qsq{blk}_{m}")
                pdq = 512 * (m % 2)
                pdk_ = f"pD{m % 2}"
                A("tensor", lambda t, pdq=pdq: t.matmul(
                    pD[0:2, pdq:pdq + 512], maskq[:, :], sqsc[:, :], start=True, stop=True),
                    reads=["maskq", "sqsc"], writes=[pdk_], tag=f"qsm{blk}_{m}")
                A("vector", lambda v, pdq=pdq: v.reduce_max(
                    mstats[0:2, 112:113], pD[0:2, pdq:pdq + 512], axis=AX.X),
                  reads=[pdk_], writes=["mred"], tag=f"qsr{blk}_{m}")
                A("vector", lambda v, m=m: v.tensor_tensor(
                    mstats[0:2, m:m + 1], mstats[0:2, m:m + 1], mstats[0:2, 112:113],
                    op=ALU.max),
                  reads=["mred", "mstats"], writes=["mstats"], tag=f"qsx{blk}_{m}")

        def qkv_k(blk):
            xv, xkeys = qkv_xv(blk)
            xln_views = [xv[:, 520 * j + 1:520 * j + 513] for j in range(NJ)]
            xk2 = [xkeys[(520 * j) // 1024] for j in range(NJ)] if len(xkeys) == 5 else xkeys
            for m in range(NJ):
                ps, pk = proj_mm(8, xln_views, xk2, m, m % 3, f"qk{blk}_{m}")
                A("scalar", lambda s, m=m, ps=ps: s.activation(
                    arS[:, 512 * m:512 * (m + 1)], ps[:, 0:512],
                    AF.Identity, bias=bT["k"][:, m:m + 1]),
                  reads=[pk, "bk"], writes=[SK[(512 * m) // 1024]], tag=f"qke{blk}_{m}")
                A("gpsimd", lambda g, m=m, blk=blk: g.dma_start(
                    out=kC[128 * m:128 * (m + 1), 512 * blk:512 * (blk + 1)],
                    in_=arS[:, 512 * m:512 * (m + 1)]),
                  reads=[SK[(512 * m) // 1024]], writes=[f"kC{m}"], kind="dma", tag=f"qkd{blk}_{m}")
                A("scalar", lambda s, m=m: s.activation(
                    sqsc[:, :], arS[:, 512 * m:512 * (m + 1)], AF.Square),
                  reads=[SK[(512 * m) // 1024]], writes=["sqsc"], tag=f"ksq{blk}_{m}")
                pdq = 512 * (m % 2)
                pdk_ = f"pD{m % 2}"
                A("tensor", lambda t, pdq=pdq: t.matmul(
                    pD[0:2, pdq:pdq + 512], maskq[:, :], sqsc[:, :], start=True, stop=True),
                    reads=["maskq", "sqsc"], writes=[pdk_], tag=f"ksm{blk}_{m}")
                A("vector", lambda v, pdq=pdq: v.reduce_max(
                    mstats[0:2, 113:114], pD[0:2, pdq:pdq + 512], axis=AX.X),
                  reads=[pdk_], writes=["mredk"], tag=f"ksr{blk}_{m}")
                A("vector", lambda v, m=m: v.tensor_tensor(
                    mstats[0:2, 16 + m:16 + m + 1], mstats[0:2, 16 + m:16 + m + 1],
                    mstats[0:2, 113:114], op=ALU.max),
                  reads=["mredk", "mstats"], writes=["mstats"], tag=f"ksx{blk}_{m}")

        def qkv_v(blk):
            xv, xkeys = qkv_xv(blk)
            xln_views = [xv[:, 520 * j + 1:520 * j + 513] for j in range(NJ)]
            xk2 = [xkeys[(520 * j) // 1024] for j in range(NJ)] if len(xkeys) == 5 else xkeys
            for m in range(NJ):
                ps, pk = proj_mm(16, xln_views, xk2, m, m % 3, f"qv{blk}_{m}")
                A("scalar", lambda s, m=m, ps=ps: s.activation(
                    arS[:, 4096 + 512 * m:4096 + 512 * (m + 1)], ps[:, 0:512],
                    AF.Identity, bias=bT["v"][:, m:m + 1]),
                  reads=[pk, "bv"], writes=[SK[(4096 + 512 * m) // 1024]], tag=f"qve{blk}_{m}")
            for u in range(4):
                cglob = 4 * blk + u
                pb = 1024 * (u % 2)
                pbk = f"pD{u % 2}"
                i0 = None
                for m in range(NJ):
                    i0 = A("tensor", lambda t, m=m, u=u, pb=pb: t.transpose(
                        pDb[:, pb + 128 * m:pb + 128 * (m + 1)],
                        arS[:, 4096 + 512 * m + 128 * u:4096 + 512 * m + 128 * (u + 1)], idb[:, :]),
                        reads=[SK[(4096 + 512 * m) // 1024], "idb"],
                        writes=([pbk] if m == 0 else []), tag=f"vt{blk}_{u}_{m}")
                pr.set_last_writer(pbk, i0)
                vtv, vtk = vt_bufs[u % 2]
                A("vector", lambda v, pb=pb, vtv=vtv: v.tensor_copy(
                    vtv.rearrange("p (h m) -> p h m", m=65)[:, :, 0:64],
                    pDb[:, pb:pb + 1024].rearrange("p (h m) -> p h m", m=64)),
                  reads=[pbk], writes=vtk, tag=f"ve{blk}_{u}")
                for q0 in range(4):
                    A("gpsimd", lambda g, q0=q0, cglob=cglob, vtv=vtv: g.dma_start(
                        out=vC[:, :].rearrange("(h p) m -> p h m", p=128)[:, 4 * q0:4 * (q0 + 1), 65 * cglob:65 * (cglob + 1)],
                        in_=vtv[:, 65 * 4 * q0:65 * 4 * (q0 + 1)].rearrange("p (h m) -> p h m", m=65)),
                      reads=vtk, writes=[f"vC{q0}"], kind="dma", tag=f"vd{blk}_{u}_{q0}")

        phase_stats("qps", tiles=False)
        xv0, xkeys0 = qkv_xv(0)
        xb_apply(0, False, "q0", xv0, xkeys0)
        for blk in range(NBLK):
            qkv_q(blk)
            qkv_k(blk)
            if blk + 1 < NBLK:
                xvn, xkeysn = qkv_xv(blk + 1)
                xb_apply(blk + 1, False, f"q{blk + 1}", xvn, xkeysn)
            qkv_v(blk)

        def cc_kv(hp):
            A("gpsimd", lambda g, hp=hp: g.collective_compute(
                "AllGather", ALU.bypass, replica_groups=PAIRS,
                ins=[kC[128 * hp:128 * (hp + 1), :].opt()],
                outs=[kG[256 * hp:256 * (hp + 1), :].opt()]),
              reads=[f"kC{hp}"], writes=[f"kG{hp}"], kind="cc", tag=f"ccK{hp}")
            A("gpsimd", lambda g, hp=hp: g.collective_compute(
                "AllGather", ALU.bypass, replica_groups=PAIRS,
                ins=[vC[256 * hp:256 * (hp + 1), :].opt()],
                outs=[vG[512 * hp:512 * (hp + 1), :].opt()]),
              reads=[f"vC{hp // 2}"], writes=[f"vG{hp}"], kind="cc", tag=f"ccV{hp}")
        A("gpsimd", lambda g, hp=0: g.collective_compute(
            "AllGather", ALU.bypass, replica_groups=PAIRS,
            ins=[kC[0:128, :].opt()], outs=[kG[0:256, :].opt()]),
          reads=["kC0"], writes=["kG0"], kind="cc", tag="ccK0")
        A("gpsimd", lambda g: g.dma_start(out=mxC[:, :], in_=mstats[0:2, 16:32]),
          reads=["mstats"], writes=["mxC"], kind="dma", tag="mxw")
        A("gpsimd", lambda g: g.collective_compute(
            "AllReduce", ALU.max, replica_groups=PAIRS,
            ins=[mxC[:, :].opt()], outs=[mxR[:, :].opt()]),
          reads=["mxC"], writes=["mxR"], kind="cc", tag="ccM")
        A("sync", lambda e: e.dma_start(out=mstats[0:2, 16:32], in_=mxR[:, :]),
          reads=["mxR", "mstats"], writes=["mstats"], kind="dma", tag="mxr")
        A("gpsimd", lambda g: g.collective_compute(
            "AllGather", ALU.bypass, replica_groups=PAIRS,
            ins=[vC[0:256, :].opt()], outs=[vG[0:512, :].opt()]),
          reads=["vC0"], writes=["vG0"], kind="cc", tag="ccV0")
        for hp in range(1, NJ):
            cc_kv(hp)
        A("vector", lambda v: v.tensor_tensor(mstats[0:2, 32:48], mstats[0:2, 0:16],
                                              mstats[0:2, 16:32], op=ALU.mult),
          reads=["mstats"], writes=["mstats"], tag="mprod")
        A("vector", lambda v: v.tensor_scalar(mstats[0:2, 32:48], mstats[0:2, 32:48],
                                              1e-30, None, op0=ALU.add),
          reads=["mstats"], writes=["mstats"], tag="mg")
        A("scalar", lambda s: s.activation(mstats[0:2, 96:112], mstats[0:2, 32:48], AF.Ln),
          reads=["mstats"], writes=["mstats"], tag="mln")
        A("scalar", lambda s: s.activation(mstats[0:2, 96:112], mstats[0:2, 96:112],
                                           AF.Exp, scale=0.5),
          reads=["mstats"], writes=["mstats"], tag="mex")
        A("vector", lambda v: v.tensor_scalar(mstats[0:2, 96:112], mstats[0:2, 96:112],
                                              0.125, None, op0=ALU.mult),
          reads=["mstats"], writes=["mstats"], tag="msc")
        A("gpsimd", lambda g: g.dma_start(
            out=mrow[0:1, 0:16].rearrange("p (m two) -> p m two", two=2)[:, :, 0:1],
            in_=mstats[0:1, 96:104], allow_slow_non_contiguous=True),
          reads=["mstats"], writes=["mrow"], kind="dma", tag="mrow0")
        A("gpsimd", lambda g: g.dma_start(
            out=mrow[0:1, 0:16].rearrange("p (m two) -> p m two", two=2)[:, :, 1:2],
            in_=mstats[1:2, 96:104], allow_slow_non_contiguous=True),
          reads=["mstats", "mrow"], writes=["mrow"], kind="dma", tag="mrow1")
        A("vector", lambda v: v.tensor_tensor(
            mrow[0:1, 16:24].rearrange("p (h one) -> p h one", one=1),
            mrow[0:1, 0:16].rearrange("p (h two) -> p h two", two=2)[:, :, 0:1],
            mrow[0:1, 0:16].rearrange("p (h two) -> p h two", two=2)[:, :, 1:2],
            op=ALU.max),
          reads=["mrow"], writes=["mrow"], tag="mpair")
        A("tensor", lambda t: t.matmul(pD[:, 0:8], negonesf[0:1, :], mrow[0:1, 16:24],
                                       start=True, stop=True),
          reads=["negonesf", "mrow"], writes=["pD0"], tag="mbc")
        A("vector", lambda v: v.tensor_copy(mne[:, :], pD[:, 0:8]),
          reads=["pD0"], writes=["mne"], tag="mne")

        vhs_off = [8192, 12352]
        av_banks = [(pC, "pCa", "pCb"), (pD, "pD0", "pD1")]
        att_pend = [None]

        def att_flush():
            if att_pend[0] is not None:
                avfn, epifn = att_pend[0]
                att_pend[0] = None
                avfn()
                if epifn is not None:
                    epifn()

        def make_epilogue(hp, s):
            pV, ka, kb = av_banks[s % 2]

            def epi():
                A("vector", lambda v, hp=hp, s=s, pV=pV: v.tensor_copy(
                    arS[0:64, 2048 * hp + 512 * s:2048 * hp + 512 * (s + 1)],
                    pV[0:64, 0:512]),
                  reads=[ka], writes=[SK[2 * hp], SK[2 * hp + 1]],
                  tag=f"ameA{hp}_{s}")
                A("vector", lambda v, hp=hp, s=s, pV=pV: v.tensor_copy(
                    arS[64:128, 2048 * hp + 512 * s:2048 * hp + 512 * (s + 1)],
                    pV[0:64, 512:1024]),
                  reads=[kb], writes=[SK[2 * hp], SK[2 * hp + 1]],
                  tag=f"ameB{hp}_{s}")
                A("vector", lambda v, pV=pV: v.tensor_copy(arWf[64:65, 10240:10752], pV[64:65, 0:512]),
                  reads=[ka], writes=wkeys(20480, 21504), tag=f"ad1{hp}_{s}")
                A("vector", lambda v, pV=pV: v.tensor_copy(arWf[64:65, 10752:11264], pV[64:65, 512:1024]),
                  reads=[kb, *wkeys(20480, 21504)], writes=wkeys(21504, 22528), tag=f"ad2{hp}_{s}")
                A("gpsimd", lambda g_, hp=hp, s=s: g_.dma_start(
                    out=arSf[2 * hp:2 * hp + 1, 8192 + 512 * s:8192 + 512 * (s + 1)],
                    in_=arWf[64:65, 10240:10752]),
                  reads=wkeys(20480, 21504), writes=[f"den{2 * hp}_{s}"],
                  kind="dma", tag=f"admA{hp}_{s}")
                A("gpsimd", lambda g_, hp=hp, s=s: g_.dma_start(
                    out=arSf[2 * hp + 1:2 * hp + 2, 8192 + 512 * s:8192 + 512 * (s + 1)],
                    in_=arWf[64:65, 10752:11264]),
                  reads=wkeys(21504, 22528), writes=[f"den{2 * hp + 1}_{s}"],
                  kind="dma", tag=f"admB{hp}_{s}")
            return epi

        for hp in range(NJ):
            p2 = hp % 2
            kc = 4096 * p2
            kkey = f"Khs{p2}"
            vc = vhs_off[p2]
            vkey = f"Vh{p2}"
            for r in range(2):
                A("sync", lambda e, hp=hp, r=r, kc=kc: e.dma_start(
                    out=arW[:, kc + 2048 * r:kc + 2048 * (r + 1)],
                    in_=kG[256 * hp + 128 * r:256 * hp + 128 * (r + 1), :]),
                  reads=[f"kG{hp}"], writes=[kkey] + wkeys(kc + 2048 * r, kc + 2048 * (r + 1)),
                  kind="dma", tag=f"akl{hp}_{r}")
                for tt in range(2):
                    A("sync", lambda e, hp=hp, r=r, tt=tt, vc=vc: e.dma_start(
                        out=arW[:, vc + 2080 * tt + 1040 * r:vc + 2080 * tt + 1040 * (r + 1)],
                        in_=vG[512 * hp + 256 * r + 128 * tt:512 * hp + 256 * r + 128 * (tt + 1), :]),
                      reads=[f"vG{hp}"],
                      writes=[vkey] + wkeys(vc + 2080 * tt + 1040 * r, vc + 2080 * tt + 1040 * (r + 1)),
                      kind="dma", tag=f"avl{hp}_{r}_{tt}")
            for s in range(4):
                pV, ka, kb = av_banks[s % 2]

                def emit_av(g, pt, ptk, hp=hp, s=s, vc=vc, vkey=vkey, pV=pV, ka=ka, kb=kb):
                    last = None
                    for tt in range(2):
                        vcol = vc + 2080 * tt + 1040 * (g // 16) + 65 * (g % 16)
                        last = A("tensor", lambda t, g=g, tt=tt, vcol=vcol, pt=pt, pV=pV: t.matmul(
                            pV[0:65, 512 * tt:512 * (tt + 1)],
                            arW[:, vcol:vcol + 65], pt[:, 512 * tt:512 * (tt + 1)],
                            start=(g == 0), stop=(g == 31)),
                            reads=[vkey, ptk] + ([ka, kb] if (g, tt) == (0, 0) else []),
                            writes=([ka, kb] if (g, tt) == (0, 0) else []),
                            tag=f"av{hp}_{s}_{g}_{tt}")
                    if g == 31:
                        pr.set_last_writer(ka, last)
                        pr.set_last_writer(kb, last)

                for g in range(32):
                    pS = pA if g % 2 == 0 else pB
                    pSk = "pA" if g % 2 == 0 else "pB"
                    pt = arW[:, 18432 + 1024 * (g % 2):18432 + 1024 * (g % 2 + 1)]
                    ptk = WK[18 + g % 2]
                    last = None
                    for tt in range(2):
                        rb = 64 * tt
                        last = A("tensor", lambda t, g=g, tt=tt, s=s, pS=pS, rb=rb, hp=hp, kc=kc: t.matmul(
                            pS[:, 512 * tt:512 * (tt + 1)],
                            arW[rb:rb + 64, kc + 128 * g:kc + 128 * (g + 1)],
                            arQ[rb:rb + 64, 2048 * hp + 512 * s:2048 * hp + 512 * (s + 1)],
                            start=True, stop=True),
                            reads=[kkey, f"Q{hp}"], writes=([pSk] if tt == 0 else []),
                            tag=f"as{hp}_{s}_{g}_{tt}")
                    pr.set_last_writer(pSk, last)
                    A("scalar", lambda sc, pS=pS, pt=pt, hp=hp: sc.activation(
                        pt, pS[:, :], AF.Exp, bias=mne[:, hp:hp + 1], scale=0.125),
                      reads=[pSk, "mne"], writes=[ptk], tag=f"ae{hp}_{s}_{g}")
                    att_flush()
                    att_pend[0] = ((lambda g=g, pt=pt, ptk=ptk, f=emit_av: f(g, pt, ptk)),
                                   (make_epilogue(hp, s) if g == 31 else None))
        att_flush()

        wo_views = [arQ[:, 8192 + 1024 * j:8192 + 1024 * (j + 1)] for j in range(6)] + \
                   [arX[:, 0:1024], arX[:, 1024:2048]]
        wo_keys = [[f"Q{4 + j // 2}"] for j in range(6)] + \
                  [[f"X{j}" for j in range(NJ)]] * 2
        for j in range(NJ):
            stg = arQf[:, 2048 + 1024 * (j % 2):2048 + 1024 * (j % 2 + 1)]
            sks = [f"Q{2 + j % 2}"]
            A("sync", lambda e, j=j, stg=stg: e.dma_start(
                out=stg, in_=ct["o"][128 * j:128 * (j + 1), :]),
              writes=sks, kind="dma", tag=f"wold{j}")
            A("vector", lambda v, j=j, stg=stg: v.tensor_copy(wo_views[j][:, :], stg),
              reads=sks, writes=wo_keys[j], tag=f"wocv{j}")

        A("vector", lambda v: v.tensor_copy(oddm[:, :], oddm[:, :]),
          writes=WK + ["Khs0", "Khs1", "Vh0", "Vh1"], reads=["oddm"], tag="fence2")

        den_keys = [f"den{h}_{s}" for h in range(NH) for s in range(4)]
        A("vector", lambda v: v.reciprocal(arSf[0:16, 8192:10240], arSf[0:16, 8192:10240]),
          reads=den_keys, writes=["rden"], tag="recip")
        for j in range(NJ):
            bb = 2048 * (j % 2)
            bk = ["Q0", "Q1"] if j % 2 == 0 else ["Q2", "Q3"]
            A("gpsimd", lambda g, j=j, bb=bb: g.dma_start(
                out=arQf[0:2, bb:bb + 2048],
                in_=arSf[2 * j:2 * (j + 1), 8192:10240]),
              reads=["rden"], writes=bk, kind="dma", tag=f"rc{j}")
            for half in range(2):
                for u in range(2):
                    A("tensor", lambda t, u=u, half=half, bb=bb: t.matmul(
                        pD[:, 512 * u:512 * (u + 1)], pairmask[:, :],
                        arQf[0:2, bb + 1024 * half + 512 * u:bb + 1024 * half + 512 * (u + 1)],
                        start=True, stop=True),
                        reads=["pairmask"] + bk,
                        writes=[f"pD{u}"], tag=f"rb{j}_{half}_{u}")
                A("vector", lambda v, j=j, half=half: v.tensor_tensor(
                    arS[:, 2048 * j + 1024 * half:2048 * j + 1024 * (half + 1)],
                    arS[:, 2048 * j + 1024 * half:2048 * j + 1024 * (half + 1)],
                    pD[:, 0:1024], op=ALU.mult),
                  reads=["pD0", "pD1", SK[2 * j], SK[2 * j + 1]],
                  writes=[SK[2 * j], SK[2 * j + 1]], tag=f"rs{j}_{half}")

        merge_views = [arS[:, 2048 * j:2048 * (j + 1)] for j in range(NJ)]
        for blk in (3, 0, 1, 2):
            mv = [merge_views[j][:, 512 * blk:512 * (blk + 1)] for j in range(NJ)]
            mk = [SK[(2048 * j + 512 * blk) // 1024] for j in range(NJ)]
            for m in range(NJ):
                ps, pk = proj_mm(0, mv, mk, m, m % 3, f"wo{blk}_{m}",
                                 w_views=wo_views, w_keys=wo_keys)
                A("scalar", lambda s, m=m, ps=ps: s.activation(
                    arS[:, 16384 + 512 * m:16384 + 512 * (m + 1)], ps[:, 0:512],
                    AF.Identity, bias=bT["o"][:, m:m + 1]),
                  reads=[pk, "bo"],
                  writes=[SK[(16384 + 512 * m) // 1024], "rden"],
                  tag=f"woe{blk}_{m}")
            for u in range(4):
                ti = 4 * blk + u
                pb = 1024 * (u % 2)
                pbk = f"pD{u % 2}"
                i0 = None
                for m in range(NJ):
                    i0 = A("tensor", lambda t, m=m, u=u, pb=pb: t.transpose(
                        pDb[:, pb + 128 * m:pb + 128 * (m + 1)],
                        arS[:, 16384 + 512 * m + 128 * u:16384 + 512 * m + 128 * (u + 1)], idb[:, :]),
                        reads=[SK[(16384 + 512 * m) // 1024], "idb"],
                        writes=([pbk] if m == 0 else []), tag=f"wot{blk}_{u}_{m}")
                pr.set_last_writer(pbk, i0)
                A("vector", lambda v, ti=ti, pb=pb: v.tensor_tensor(
                    resid[ti][:, :], resid[ti][:, :], pDb[:, pb:pb + 1024], op=ALU.add),
                  reads=[pbk, f"res{ti}"], writes=[f"res{ti}"], tag=f"woa{blk}_{u}")
                ln_stats(ti, u % 2, f"f2pre{ti}")
            if blk == 3:
                A("gpsimd", lambda g: g.dma_start(out=bC[:, :], in_=resid[15][127:128, :]),
                  reads=["res15"], writes=["bC"], kind="dma", tag="bcw")
                A("gpsimd", lambda g: g.collective_compute(
                    "AllGather", ALU.bypass, replica_groups=PAIRS,
                    ins=[bC[:, :].opt()], outs=[bG[:, :].opt()]),
                  reads=["bC"], writes=["bG"], kind="cc", tag="ccB")
                A("sync", lambda e: e.dma_start(out=arWf[0:1, 4096:5120], in_=bG[0:1, :]),
                  reads=["bG"], writes=wkeys(8192, 10240), kind="dma", tag="bgr")

        A("vector", lambda v: v.tensor_scalar(resb, arWf[0:1, 4096:5120],
                                              oddm[0:1, 0:1], None, op0=ALU.mult),
          reads=[*wkeys(8192, 10240), "oddm"], writes=["resb", SK[14], SK[15]], tag="bmask")

        ffn_phase("f2", pre_stats=True, from_cache=True)

        for i in range(NTL):
            A("sync", lambda e, i=i: e.dma_start(out=y_out[128 * i:128 * (i + 1), :], in_=resid[i][:, :]),
              reads=[f"res{i}"], kind="dma", tag=f"st{i}")

        pr.emit(ex)
    return nc


_NC_CACHE = []


def kernel(L0_attn_x, L0_attn_wq_c, L0_attn_wq_b, L0_attn_wk_c, L0_attn_wk_b,
           L0_attn_wv_c, L0_attn_wv_b, L0_attn_wo_c, L0_attn_wo_b,
           L0_ffn_W_r, L0_ffn_W_k, L0_ffn_W_v, L0_ffn_mu_r, L0_ffn_mu_k):
    x = np.asarray(L0_attn_x, dtype=np.float32)
    B, TT, DD = x.shape
    assert (B, TT, DD) == (4, 4096, 1024)

    idx = (np.arange(D)[None, :] - np.arange(D)[:, None]) % D
    def circ(wc):
        return np.ascontiguousarray(np.asarray(wc, np.float32)[idx])
    def fm8(v):
        return np.ascontiguousarray(np.asarray(v, np.float32).reshape(NJ, 128).T)

    common = {
        "idf": np.eye(128, dtype=np.float32),
        "ct_q": circ(L0_attn_wq_c), "ct_k": circ(L0_attn_wk_c),
        "ct_v": circ(L0_attn_wv_c), "ct_o": circ(L0_attn_wo_c),
        "b_q": fm8(L0_attn_wq_b), "b_k": fm8(L0_attn_wk_b),
        "b_v": fm8(L0_attn_wv_b), "b_o": fm8(L0_attn_wo_b),
        "mu_r": fm8(L0_ffn_mu_r), "mu_k": fm8(L0_ffn_mu_k),
        "wrT": np.ascontiguousarray(np.asarray(L0_ffn_W_r, np.float32).T),
        "wkT": np.ascontiguousarray(np.asarray(L0_ffn_W_k, np.float32).T),
        "wvT": np.ascontiguousarray(np.asarray(L0_ffn_W_v, np.float32).T),
    }
    zrow = np.zeros((1, D), np.float32)
    in_maps = []
    for c in range(8):
        seq, half = c // 2, c % 2
        t0 = half * T
        m = dict(common)
        m["x"] = np.ascontiguousarray(x[seq, t0:t0 + T, :])
        m["xbnd"] = np.ascontiguousarray(x[seq, t0 - 1:t0, :]) if half else zrow
        m["oddmask"] = np.array([[float(half)]], np.float32)
        in_maps.append(m)

    if not _NC_CACHE:
        _NC_CACHE.append(build_nc())
    nc = _NC_CACHE[0]
    res = run_bass_kernel_spmd(nc, in_maps, core_ids=list(range(8)))
    y = np.empty((4, 4096, 1024), np.float32)
    for c in range(8):
        seq, half = c // 2, c % 2
        y[seq, half * T:(half + 1) * T, :] = np.asarray(res.results[c]["y"])
    return y


# version 35
# speedup vs baseline: 1.0161x; 1.0161x over previous
import contextlib
import numpy as np
import concourse.bass as bass
import concourse.mybir as mybir
from concourse.bass_utils import run_bass_kernel_spmd

F32 = mybir.dt.float32
BF16 = mybir.dt.bfloat16
AF = mybir.ActivationFunctionType
ALU = mybir.AluOpType
AX = mybir.AxisListType

D = 1024
T = 2048
NH = 16
DH = 64
NJ = 8
NTL = 16
NBLK = 4
BLK = 512
EPS = 1e-5
PAIRS = [[0, 1], [2, 3], [4, 5], [6, 7]]

ENGINES = ("sync", "vector", "scalar", "tensor", "gpsimd")
N_DMA_LANES = 8


class _Op:
    __slots__ = ("idx", "eng", "fn", "deps", "kind", "sem", "val", "needs_inc", "tag")

    def __init__(self, idx, eng, fn, deps, kind, tag):
        self.idx = idx
        self.eng = eng
        self.fn = fn
        self.deps = deps
        self.kind = kind
        self.sem = None
        self.val = None
        self.needs_inc = False
        self.tag = tag


class Prog:
    def __init__(self, nc):
        self.nc = nc
        self.ops = []
        self.last_w = {}
        self.readers = {}

    def add(self, eng, fn, reads=(), writes=(), kind="compute", tag=""):
        deps = set()
        for k in reads:
            i = self.last_w.get(k)
            if i is not None:
                deps.add(i)
        for k in writes:
            i = self.last_w.get(k)
            if i is not None:
                deps.add(i)
            for r in self.readers.get(k, ()):
                deps.add(r)
        idx = len(self.ops)
        self.ops.append(_Op(idx, eng, fn, deps, kind, tag))
        for k in reads:
            self.readers.setdefault(k, []).append(idx)
        for k in writes:
            self.last_w[k] = idx
            self.readers[k] = []
        return idx

    def set_last_writer(self, key, idx):
        self.last_w[key] = idx
        self.readers[key] = []

    def emit(self, ex):
        nc = self.nc
        ops = self.ops
        for op in ops:
            tdeps = [d for d in op.deps if ops[d].eng == "tensor"]
            if len(tdeps) > 1:
                keep = max(tdeps)
                for d in tdeps:
                    if d != keep:
                        op.deps.discard(d)
        for op in ops:
            for d in op.deps:
                ops[d].needs_inc = True
        for op in ops:
            if op.kind in ("dma", "cc"):
                op.needs_inc = True

        tick = {}
        dma_lanes = {}
        any_compute = {e: any(o.eng == e and o.kind == "compute" and o.needs_inc
                              for o in ops) for e in ENGINES}
        n_dma = {e: sum(1 for o in ops if o.eng == e and o.kind == "dma")
                 for e in ENGINES}
        for e in ENGINES:
            if any_compute[e]:
                tick[e] = ex.enter_context(nc.semaphore(f"tick_{e}"))
            dma_lanes[e] = [ex.enter_context(nc.semaphore(f"dl_{e}{i}"))
                            for i in range(min(N_DMA_LANES, n_dma[e]))]
        n_cc = sum(1 for o in ops if o.kind == "cc")
        cc_sems = [ex.enter_context(nc.semaphore(f"cc{i}")) for i in range(n_cc)]

        cnt = {}
        dma_i = {e: 0 for e in ENGINES}
        cc_cnt = 0
        lane_prev = {}
        for op in ops:
            if op.kind == "compute":
                if op.needs_inc:
                    s = tick[op.eng]
                    cnt[s.num] = cnt.get(s.num, 0) + 1
                    op.sem, op.val = s, cnt[s.num]
            elif op.kind == "dma":
                lanes = dma_lanes[op.eng]
                s = lanes[dma_i[op.eng] % len(lanes)]
                dma_i[op.eng] += 1
                cnt[s.num] = cnt.get(s.num, 0) + 16
                op.sem, op.val = s, cnt[s.num]
                if s.num in lane_prev:
                    op.deps.add(lane_prev[s.num])
                lane_prev[s.num] = op.idx
            elif op.kind == "cc":
                op.sem, op.val = cc_sems[cc_cnt], 1
                cc_cnt += 1

        snapshots = {}
        know = {e: {} for e in ENGINES}
        waits_for_op = [None] * len(ops)
        for op in ops:
            k = know[op.eng]
            w = []
            for d in sorted(op.deps):
                dop = ops[d]
                if dop.sem is None:
                    raise RuntimeError(f"dep of {op.tag or op.idx} on "
                                       f"{dop.tag or d} has no sem")
                if k.get(dop.sem.num, 0) >= dop.val:
                    continue
                w.append((dop.sem, dop.val))
                k[dop.sem.num] = dop.val
                snap = snapshots.get((dop.sem.num, dop.val))
                if snap:
                    for s2, v2 in snap.items():
                        if k.get(s2, 0) < v2:
                            k[s2] = v2
            waits_for_op[op.idx] = w
            if op.needs_inc:
                snap = dict(k)
                snap[op.sem.num] = op.val
                snapshots[(op.sem.num, op.val)] = snap

        by_eng = {e: [o for o in ops if o.eng == e] for e in ENGINES}
        final = dict(cnt)
        for s in cc_sems:
            final[s.num] = 1
        sem_by_id = {}
        for e in ENGINES:
            if e in tick:
                sem_by_id[tick[e].num] = tick[e]
            for s in dma_lanes[e]:
                sem_by_id[s.num] = s
        for s in cc_sems:
            sem_by_id[s.num] = s

        block = ex.enter_context(nc.Block())

        def run_engine(engobj, eng):
            for op in by_eng[eng]:
                for sem, val in waits_for_op[op.idx]:
                    engobj.wait_ge(sem, val)
                inst = op.fn(engobj)
                if op.needs_inc:
                    if op.kind == "dma":
                        inst.then_inc(op.sem, 16)
                    elif op.kind == "cc":
                        inst.then_inc(op.sem)
                    else:
                        inst.then_inc(op.sem, 1)

        @block.sync
        def _(eng):
            run_engine(eng, "sync")
            for sid, v in sorted(final.items()):
                eng.wait_ge(sem_by_id[sid], v)

        @block.vector
        def _(eng):
            run_engine(eng, "vector")

        @block.scalar
        def _(eng):
            run_engine(eng, "scalar")

        @block.tensor
        def _(eng):
            run_engine(eng, "tensor")

        @block.gpsimd
        def _(eng):
            run_engine(eng, "gpsimd")


def build_nc():
    nc = bass.Bass("TRN2", target_bir_lowering=False)

    x_in = nc.dram_tensor("x", [T, D], F32, kind="ExternalInput").ap()
    xbnd = nc.dram_tensor("xbnd", [1, D], F32, kind="ExternalInput").ap()
    oddm_in = nc.dram_tensor("oddmask", [1, 1], F32, kind="ExternalInput").ap()
    idf = nc.dram_tensor("idf", [128, 128], F32, kind="ExternalInput").ap()
    ct = {w: nc.dram_tensor(f"ct_{w}", [D, D], F32, kind="ExternalInput").ap()
          for w in ("q", "k", "v", "o")}
    bvec = {w: nc.dram_tensor(f"b_{w}", [128, NJ], F32, kind="ExternalInput").ap()
            for w in ("q", "k", "v", "o")}
    mu_in = {w: nc.dram_tensor(f"mu_{w}", [128, NJ], F32, kind="ExternalInput").ap()
             for w in ("r", "k")}
    wT = {w: nc.dram_tensor(f"w{w}T", [D, D], F32, kind="ExternalInput").ap()
          for w in ("r", "k", "v")}
    y_out = nc.dram_tensor("y", [T, D], F32, kind="ExternalOutput").ap()

    kC = nc.dram_tensor("kC", [D, T], BF16).ap()
    kG = nc.dram_tensor("kG", [2 * D, T], BF16).ap()
    vC = nc.dram_tensor("vC", [NH * 128, 1040], BF16).ap()
    vG = nc.dram_tensor("vG", [2 * NH * 128, 1040], BF16).ap()
    bC = nc.dram_tensor("bC", [1, D], F32).ap()
    mxC = nc.dram_tensor("mxC", [2, 16], F32).ap()
    mxR = nc.dram_tensor("mxR", [2, 16], F32).ap()
    bG = nc.dram_tensor("bG", [2, D], F32).ap()
    wbf = nc.dram_tensor("wbf", [128, 24576], BF16).ap()

    ex = contextlib.ExitStack()
    with ex:
        def S(name, shape, dt=F32):
            return ex.enter_context(nc.sbuf_tensor(name, shape, dt))
        def P(name, shape, dt=F32):
            return ex.enter_context(nc.psum_tensor(name, shape, dt))

        resid = [S(f"res{i}", [128, D]) for i in range(NTL)]
        arW = S("arW", [128, 24576], BF16)
        arS = S("arS", [128, 20480], BF16)
        arQ = S("arQ", [128, 16384], BF16)
        arX = S("arX", [128, 4160], BF16)

        bn12 = [S(f"bn12_{i}", [128, 12]) for i in range(2)]
        stb = S("stb", [128, 128])
        bnb12 = S("bnb12", [1, 12])
        sttb = S("sttb", [1, 32])

        muT = {w: S(f"muT_{w}", [128, NJ]) for w in ("r", "k")}
        bT = {w: S(f"bT_{w}", [128, NJ]) for w in ("q", "k", "v", "o")}
        idb = S("idb", [128, 128], BF16)
        onesb1 = S("onesb1", [1, 1], BF16)
        negonesf = S("negonesf", [1, 128])
        onesf = S("onesf", [1, 128])
        maskq = S("maskq", [128, 2], BF16)
        pairmask = S("pairmask", [2, 128])
        mrow = S("mrow", [1, 24])
        mstats = S("mstats", [2, 128])
        sqsc = S("sqsc", [128, 512], BF16)
        mne = S("mne", [128, 8])
        oddm = S("oddm", [1, 1])
        halfc = S("halfc", [128, 1])
        xtm = [S(f"xtm{i}", [128, D], BF16) for i in range(2)]

        arWf = arW[:, :].bitcast(F32)
        arSf = arS[:, :].bitcast(F32)
        arQf = arQ[:, :].bitcast(F32)
        resb = arSf[32:33, 7168:8192]
        xlnb = arS[0:1, 14336:15360]
        convst = [arQf[:, 1024 * i:1024 * (i + 1)] for i in range(2)]

        pA = P("pA", [128, 1024])
        pB = P("pB", [128, 1024])
        pC = P("pC", [128, 1024])
        pD = P("pD", [128, 1024])
        pDb = pD[:, :].bitcast(BF16)

        pr = Prog(nc)
        A = pr.add

        def wslot(i, n=1):
            return arW[:, 1024 * i:1024 * (i + 1 + (n - 1))]
        WK = [f"W{i}" for i in range(24)]
        SK = [f"S{i}" for i in range(20)]

        def wkeys(c0, c1):
            return [WK[i] for i in range(c0 // 1024, (c1 + 1023) // 1024)]
        def skeys(c0, c1):
            return [SK[i] for i in range(c0 // 1024, (c1 + 1023) // 1024)]

        for i in range(4):
            A("sync", lambda e, i=i: e.dma_start(out=resid[i][:, :], in_=x_in[128 * i:128 * (i + 1), :]),
              writes=[f"res{i}"], kind="dma")
        A("sync", lambda e: e.dma_start(out=resb, in_=xbnd), writes=["resb"], kind="dma")

        A("sync", lambda e: e.dma_start(out=convst[0][:, 0:128], in_=idf), writes=["Q0"], kind="dma")
        A("vector", lambda v: v.tensor_copy(idb[:, :], convst[0][:, 0:128]), reads=["Q0"], writes=["idb"])
        A("vector", lambda v: v.memset(onesb1[:, :], 1.0), writes=["onesb1"])
        A("vector", lambda v: v.memset(halfc[:, :], 0.5), writes=["halfc"])
        A("vector", lambda v: v.memset(onesf[:, :], 1.0), writes=["onesf"])
        A("vector", lambda v: v.memset(negonesf[:, :], -1.0), writes=["negonesf"])
        A("vector", lambda v: v.memset(maskq[:, :], 0.0), writes=["maskq"])
        A("vector", lambda v: v.memset(maskq[0:64, 0:1], 1.0), writes=["maskq"])
        A("vector", lambda v: v.memset(maskq[64:128, 1:2], 1.0), writes=["maskq"])
        A("vector", lambda v: v.memset(pairmask[:, :], 0.0), writes=["pairmask"])
        A("vector", lambda v: v.memset(pairmask[0:1, 0:64], 1.0), writes=["pairmask"])
        A("gpsimd", lambda g: g.dma_start(out=pairmask[1:2, 64:128], in_=onesf[0:1, 0:64]),
          reads=["onesf", "pairmask"], writes=["pairmask"], kind="dma")
        for w in ("r", "k"):
            A("sync", lambda e, w=w: e.dma_start(out=muT[w][:, :], in_=mu_in[w]), writes=[f"mu{w}"], kind="dma")
        for w in ("q", "k", "v", "o"):
            A("sync", lambda e, w=w: e.dma_start(out=bT[w][:, :], in_=bvec[w]),
              writes=[f"b{w}"], kind="dma")
        A("sync", lambda e: e.dma_start(out=oddm[:, :], in_=oddm_in), writes=["oddm"], kind="dma")

        conv_n = [0]

        def load_weight_chunks(dram_ap, slot0, tag):
            for j in range(NJ):
                ci = conv_n[0] % 2
                conv_n[0] += 1
                A("sync", lambda e, j=j, ci=ci, d=dram_ap: e.dma_start(
                    out=convst[ci], in_=d[128 * j:128 * (j + 1), :]),
                  writes=[f"Q{ci}"], kind="dma", tag=f"{tag}ld{j}")
                A("scalar", lambda v, j=j, ci=ci, s=slot0: v.copy(
                    wslot(s + j)[:, :], convst[ci]),
                  reads=[f"Q{ci}"], writes=[WK[slot0 + j]], kind="compute", tag=f"{tag}cv{j}")

        INT32 = mybir.dt.int32

        def ln_stats(ti, parity, tag):
            rk = f"res{ti}"
            b = bn12[parity]
            bk = f"bn{parity}"
            A("vector", lambda v: v.bn_stats(b[:, 0:6], resid[ti][:, 0:512]),
              reads=[rk], writes=[bk], tag=f"{tag}bnA")
            A("vector", lambda v: v.bn_stats(b[:, 6:12], resid[ti][:, 512:1024]),
              reads=[rk, bk], writes=[bk], tag=f"{tag}bnB")
            A("vector", lambda v, ti=ti: v.bn_aggr(stb[:, 8 * ti:8 * ti + 2], b[:, :]),
              reads=[bk], writes=[f"stbT{ti}"], tag=f"{tag}ag")

        def phase_stats(tag, tiles=True):
            if tiles:
                for ti in range(NTL):
                    ln_stats(ti, ti % 2, f"{tag}st{ti}")
            s3 = stb[:, :].rearrange("p (t c) -> p t c", c=8)
            def col(b, c):
                return s3[:, 4 * b:4 * (b + 1), c:c + 1].squeeze(2)
            for b in range(4):
                tk = [f"stbT{4 * b + u}" for u in range(4)]
                fk = f"stbF{b}"
                A("vector", lambda v, b=b: v.tensor_scalar(col(b, 2), col(b, 1), EPS, None, op0=ALU.add),
                  reads=tk, writes=[fk], tag=f"{tag}ve{b}")
                A("scalar", lambda s, b=b: s.activation(col(b, 3), col(b, 2), AF.Ln),
                  reads=[fk], writes=[fk], tag=f"{tag}ln{b}")
                A("scalar", lambda s, b=b: s.activation(col(b, 3), col(b, 3), AF.Exp, scale=-0.5),
                  reads=[fk], writes=[fk], tag=f"{tag}ex{b}")

        def ln_bnd(tag):
            A("vector", lambda v: v.bn_stats(bnb12[:, 0:6], resb[0:1, 0:512]),
              reads=["resb"], writes=["bnb"], tag=f"{tag}bnA")
            A("vector", lambda v: v.bn_stats(bnb12[:, 6:12], resb[0:1, 512:1024]),
              reads=["resb", "bnb"], writes=["bnb"], tag=f"{tag}bnB")
            A("vector", lambda v: v.bn_aggr(sttb[:, 0:2], bnb12[:, :]),
              reads=["bnb"], writes=["sttb"], tag=f"{tag}ag")
            A("vector", lambda v: v.tensor_scalar(sttb[:, 2:3], sttb[:, 1:2], EPS, None, op0=ALU.add),
              reads=["sttb"], writes=["sttb"], tag=f"{tag}ve")
            A("scalar", lambda s: s.activation(sttb[:, 3:4], sttb[:, 2:3], AF.Ln),
              reads=["sttb"], writes=["sttb"], tag=f"{tag}l")
            A("scalar", lambda s: s.activation(sttb[:, 3:4], sttb[:, 3:4], AF.Exp, scale=-0.5),
              reads=["sttb"], writes=["sttb"], tag=f"{tag}e")
            A("vector", lambda v: v.tensor_scalar(xlnb, resb,
                                                  sttb[:, 0:1], sttb[:, 3:4],
                                                  op0=ALU.subtract, op1=ALU.mult),
              reads=["resb", "sttb"], writes=[SK[14]], tag=f"{tag}ap")
            for j in range(NJ):
                i0 = A("tensor", lambda t, j=j: t.matmul(pD[:, j:j + 1],
                                                         xlnb[0:1, 128 * j:128 * (j + 1)],
                                                         onesb1[0:1, 0:1],
                                                         start=True, stop=True),
                       reads=[SK[14], "onesb1"], writes=[] if j else ["pD0"], tag=f"{tag}mm{j}")
            pr.set_last_writer("pD0", i0)
            A("vector", lambda v: v.tensor_copy(
                arX[:, 0:4160].rearrange("p (j c) -> p j c", c=520)[:, :, 0:1],
                pD[:, 0:8].rearrange("p (j one) -> p j one", one=1)),
              reads=["pD0"], writes=[f"X{j}" for j in range(NJ)], tag=f"{tag}ev")

        def xb_apply(blk, with_bnd, tag, xv, xkeys):
            if blk > 0 and with_bnd:
                A("vector", lambda v: v.tensor_copy(
                    xv.rearrange("p (j c) -> p j c", c=520)[:, :, 0:1],
                    xv.rearrange("p (j c) -> p j c", c=520)[:, :, 512:513]),
                  reads=xkeys, writes=xkeys, tag=f"{tag}bc")
            for u in range(4):
                ti = 4 * blk + u
                par = u % 2
                A("vector", lambda v, ti=ti, par=par: v.tensor_scalar(
                    xtm[par][:, :], resid[ti][:, :], stb[:, 8 * ti:8 * ti + 1],
                    stb[:, 8 * ti + 3:8 * ti + 4], op0=ALU.subtract, op1=ALU.mult),
                  reads=[f"res{ti}", f"stbF{blk}"], writes=[f"xtm{par}"], tag=f"{tag}ap{u}")
                pb = 1024 * (u % 2)
                pbk = f"pD{u % 2}"
                i0 = None
                for j in range(NJ):
                    ii = A("tensor", lambda t, j=j, par=par, pb=pb: t.transpose(
                        pDb[:, pb + 128 * j:pb + 128 * (j + 1)], xtm[par][:, 128 * j:128 * (j + 1)], idb[:, :]),
                        reads=[f"xtm{par}", "idb"], writes=([pbk] if j == 0 else []), tag=f"{tag}tr{u}_{j}")
                    i0 = ii
                pr.set_last_writer(pbk, i0)
                A("scalar", lambda v, u=u, pb=pb, xv=xv: v.copy(
                    xv.rearrange("p (j c) -> p j c", c=520)[:, :, 1 + 128 * u:1 + 128 * (u + 1)],
                    pDb[:, pb:pb + 1024].rearrange("p (j c) -> p j c", c=128)),
                  reads=[pbk], writes=xkeys, tag=f"{tag}te{u}")



        psum_pool = [pA, pB, pC]
        psum_keys = ["pA", "pB", "pC"]

        def proj_mm(wslot0, rhs_views, rhs_keys, m, pi, tag, w_views=None, w_keys=None):
            ps = psum_pool[pi]
            pk = psum_keys[pi]
            last = None
            for j in range(NJ):
                wv = w_views[j] if w_views is not None else wslot(wslot0 + j)
                wk = w_keys[j] if w_keys is not None else [WK[wslot0 + j]]
                last = A("tensor", lambda t, j=j, m=m, ps=ps, wv=wv: t.matmul(
                    ps[:, 0:512], wv[:, 128 * m:128 * (m + 1)],
                    rhs_views[j], start=(j == 0), stop=(j == NJ - 1)),
                    reads=wk + [rhs_keys[j]],
                    writes=([pk] if j == 0 else []), tag=f"{tag}m{j}")
            pr.set_last_writer(pk, last)
            return ps, pk

        def ffn_phase(tag, pre_stats=False, from_cache=False):
            if tag == "f1":
                for i in range(4, NTL):
                    A("sync", lambda e, i=i: e.dma_start(
                        out=resid[i][:, :], in_=x_in[128 * i:128 * (i + 1), :]),
                      writes=[f"res{i}"], kind="dma")
            if from_cache:
                for i in range(24):
                    A("sync", lambda e, i=i: e.dma_start(
                        out=wslot(i)[:, :], in_=wbf[:, 1024 * i:1024 * (i + 1)]),
                      reads=["wbf"], writes=[WK[i]], kind="dma", tag=f"{tag}wc{i}")
            else:
                load_weight_chunks(wT["r"], 0, f"{tag}wr")
                load_weight_chunks(wT["k"], 8, f"{tag}wk")
                load_weight_chunks(wT["v"], 16, f"{tag}wv")
            ln_bnd(f"{tag}lb")

            def bufs(blk):
                if blk % 2 == 0:
                    xr_v = [arS[:, 4096 + 512 * j:4096 + 512 * (j + 1)] for j in range(NJ)]
                    xr_k = [SK[(4096 + 512 * j) // 1024] for j in range(NJ)]
                    xk_v = [arS[:, 8192 + 512 * j:8192 + 512 * (j + 1)] for j in range(NJ)]
                    xk_k = [SK[(8192 + 512 * j) // 1024] for j in range(NJ)]
                else:
                    xr_v = [arQ[:, 4096 + 512 * j:4096 + 512 * (j + 1)] for j in range(NJ)]
                    xr_k = [f"Q{2 + (512 * j) // 2048}" for j in range(NJ)]
                    xk_v = [arQ[:, 8192 + 512 * j:8192 + 512 * (j + 1)] for j in range(NJ)]
                    xk_k = [f"Q{4 + (512 * j) // 2048}" for j in range(NJ)]
                return xr_v, xr_k, xk_v, xk_k

            out_v = [arQ[:, 12288 + 512 * j:12288 + 512 * (j + 1)] for j in range(NJ)]
            out_k = [f"Q{6 + (512 * j) // 2048}" for j in range(NJ)]

            def front_apply(blk):
                xb_apply(blk, True, f"{tag}b{blk}", arX[:, 0:4160],
                         [f"X{j}" for j in range(NJ)])
                A("gpsimd", lambda v: v.tensor_tensor(
                    arS[:, 0:4096].rearrange("p (j c) -> p j c", c=512),
                    arX[:, 0:4160].rearrange("p (j c) -> p j c", c=520)[:, :, 1:513],
                    arX[:, 0:4160].rearrange("p (j c) -> p j c", c=520)[:, :, 0:512],
                    op=ALU.subtract),
                  reads=[f"X{j}" for j in range(NJ)], writes=[SK[0], SK[1], SK[2], SK[3]],
                  tag=f"{tag}df{blk}")

            def front_mix(blk):
                xr_v, xr_k, xk_v, xk_k = bufs(blk)
                for j in range(NJ):
                    dv = arS[:, 512 * j:512 * (j + 1)]
                    dk = SK[(512 * j) // 1024]
                    A("vector", lambda v, j=j, dv=dv, xr=xr_v[j]: v.scalar_tensor_tensor(
                        xr, dv, muT["r"][:, j:j + 1], arX[:, 520 * j:520 * j + 512],
                        op0=ALU.mult, op1=ALU.add),
                      reads=[dk, "mur", f"X{j}"], writes=[xr_k[j]], tag=f"{tag}xr{blk}_{j}")
                    A("vector", lambda v, j=j, dv=dv, xk=xk_v[j]: v.scalar_tensor_tensor(
                        xk, dv, muT["k"][:, j:j + 1], arX[:, 520 * j:520 * j + 512],
                        op0=ALU.mult, op1=ALU.add),
                      reads=[dk, "muk", f"X{j}"], writes=[xk_k[j]], tag=f"{tag}xk{blk}_{j}")

            def back_r(blk):
                xr_v, xr_k, xk_v, xk_k = bufs(blk)
                for m in range(NJ):
                    ps, pk = proj_mm(0, xr_v, xr_k, m, m % 3, f"{tag}r{blk}_{m}")
                    A("scalar", lambda s, m=m, ps=ps: s.activation(
                        arS[:, 12288 + 512 * m:12288 + 512 * (m + 1)], ps[:, 0:512],
                        AF.Tanh, scale=0.5),
                      reads=[pk], writes=[SK[(12288 + 512 * m) // 1024]], tag=f"{tag}sg{blk}_{m}")
                    A("scalar", lambda s, m=m: s.activation(
                        arS[:, 12288 + 512 * m:12288 + 512 * (m + 1)],
                        arS[:, 12288 + 512 * m:12288 + 512 * (m + 1)],
                        AF.Identity, scale=0.5, bias=halfc[:, 0:1]),
                      reads=[SK[(12288 + 512 * m) // 1024], "halfc"],
                      writes=[SK[(12288 + 512 * m) // 1024]], tag=f"{tag}sa{blk}_{m}")

            def back_k(blk):
                xr_v, xr_k, xk_v, xk_k = bufs(blk)
                for m in range(NJ):
                    ps, pk = proj_mm(8, xk_v, xk_k, m, m % 3, f"{tag}k{blk}_{m}")
                    kv = arS[:, 16384 + 512 * m:16384 + 512 * (m + 1)]
                    kk = SK[(16384 + 512 * m) // 1024]
                    A("scalar", lambda s, ps=ps, kv=kv: s.activation(kv, ps[:, 0:512], AF.Relu),
                      reads=[pk], writes=[kk], tag=f"{tag}rl{blk}_{m}")
                    A("gpsimd", lambda v, kv=kv: v.tensor_tensor(kv, kv, kv, op=ALU.mult),
                      reads=[kk], writes=[kk], tag=f"{tag}sq{blk}_{m}")

            def back_v(blk):
                ka_v = [arS[:, 16384 + 512 * j:16384 + 512 * (j + 1)] for j in range(NJ)]
                ka_k = [SK[(16384 + 512 * j) // 1024] for j in range(NJ)]
                for m in range(NJ):
                    ps, pk = proj_mm(16, ka_v, ka_k, m, m % 3, f"{tag}v{blk}_{m}")
                    A("vector", lambda v, m=m, ps=ps: v.scalar_tensor_tensor(
                        out_v[m], ps[:, 0:512], halfc[:, 0:1],
                        arS[:, 12288 + 512 * m:12288 + 512 * (m + 1)],
                        op0=ALU.mult, op1=ALU.mult),
                      reads=[pk, SK[(12288 + 512 * m) // 1024], "halfc"],
                      writes=[out_k[m]], tag=f"{tag}o{blk}_{m}")
                for u in range(4):
                    ti = 4 * blk + u
                    pb = 1024 * (u % 2)
                    pbk = f"pD{u % 2}"
                    i0 = None
                    for m in range(NJ):
                        i0 = A("tensor", lambda t, m=m, u=u, pb=pb: t.transpose(
                            pDb[:, pb + 128 * m:pb + 128 * (m + 1)],
                            out_v[m][:, 128 * u:128 * (u + 1)], idb[:, :]),
                            reads=[out_k[m], "idb"],
                            writes=([pbk] if m == 0 else []), tag=f"{tag}ot{blk}_{u}_{m}")
                    pr.set_last_writer(pbk, i0)
                    A("vector", lambda v, ti=ti, pb=pb: v.tensor_tensor(
                        resid[ti][:, :], resid[ti][:, :], pDb[:, pb:pb + 1024], op=ALU.add),
                      reads=[pbk, f"res{ti}"], writes=[f"res{ti}"], tag=f"{tag}ra{blk}_{u}")
                    if tag == "f1":
                        ln_stats(ti, u % 2, f"qpre{ti}")

            phase_stats(f"{tag}ps", tiles=not pre_stats)
            front_apply(0)
            front_mix(0)
            for blk in range(NBLK):
                back_r(blk)
                if blk + 1 < NBLK:
                    front_apply(blk + 1)
                back_k(blk)
                if blk + 1 < NBLK:
                    front_mix(blk + 1)
                back_v(blk)

        ffn_phase("f1")

        for i in range(24):
            A("sync", lambda e, i=i: e.dma_start(
                out=wbf[:, 1024 * i:1024 * (i + 1)], in_=wslot(i)[:, :]),
              reads=[WK[i]], writes=["wbf"], kind="dma", tag=f"wsv{i}")

        for wi, w in enumerate(("q", "k", "v")):
            load_weight_chunks(ct[w], 8 * wi, f"qkv_{w}")
        vt_bufs = [(arS[:, 8192:9232], [SK[8], SK[9]]), (arS[:, 14400:15440], [SK[14], SK[15]])]
        A("vector", lambda v: v.memset(arS[:, 8192:9232], 1.0),
          writes=[SK[8], SK[9]], tag="vtones0")
        A("vector", lambda v: v.memset(arS[:, 14400:15440], 1.0),
          writes=[SK[14], SK[15]], tag="vtones1")
        A("vector", lambda v: v.memset(mstats[:, 0:32], 0.0), writes=["mstats"], tag="macc0")

        def qkv_xv(blk):
            if blk % 2 == 0:
                return arX[:, 0:4160], [f"X{j}" for j in range(NJ)]
            return arS[:, 10240:14400], [SK[10 + i] for i in range(5)]

        def qkv_q(blk):
            xv, xkeys = qkv_xv(blk)
            xln_views = [xv[:, 520 * j + 1:520 * j + 513] for j in range(NJ)]
            xk2 = [xkeys[(520 * j) // 1024] for j in range(NJ)] if len(xkeys) == 5 else xkeys
            for m in range(NJ):
                ps, pk = proj_mm(0, xln_views, xk2, m, m % 3, f"qq{blk}_{m}")
                A("scalar", lambda s, m=m, blk=blk, ps=ps: s.activation(
                    arQ[:, 2048 * m + 512 * blk:2048 * m + 512 * (blk + 1)], ps[:, 0:512],
                    AF.Identity, bias=bT["q"][:, m:m + 1]),
                  reads=[pk, "bq"], writes=[f"Q{m}"], tag=f"qe{blk}_{m}")
                A("scalar", lambda s, m=m, blk=blk: s.activation(
                    sqsc[:, :], arQ[:, 2048 * m + 512 * blk:2048 * m + 512 * (blk + 1)],
                    AF.Square),
                  reads=[f"Q{m}"], writes=["sqsc"], tag=f"qsq{blk}_{m}")
                pdq = 512 * (m % 2)
                pdk_ = f"pD{m % 2}"
                A("tensor", lambda t, pdq=pdq: t.matmul(
                    pD[0:2, pdq:pdq + 512], maskq[:, :], sqsc[:, :], start=True, stop=True),
                    reads=["maskq", "sqsc"], writes=[pdk_], tag=f"qsm{blk}_{m}")
                A("vector", lambda v, pdq=pdq: v.reduce_max(
                    mstats[0:2, 112:113], pD[0:2, pdq:pdq + 512], axis=AX.X),
                  reads=[pdk_], writes=["mred"], tag=f"qsr{blk}_{m}")
                A("vector", lambda v, m=m: v.tensor_tensor(
                    mstats[0:2, m:m + 1], mstats[0:2, m:m + 1], mstats[0:2, 112:113],
                    op=ALU.max),
                  reads=["mred", "mstats"], writes=["mstats"], tag=f"qsx{blk}_{m}")

        def qkv_k(blk):
            xv, xkeys = qkv_xv(blk)
            xln_views = [xv[:, 520 * j + 1:520 * j + 513] for j in range(NJ)]
            xk2 = [xkeys[(520 * j) // 1024] for j in range(NJ)] if len(xkeys) == 5 else xkeys
            for m in range(NJ):
                ps, pk = proj_mm(8, xln_views, xk2, m, m % 3, f"qk{blk}_{m}")
                A("scalar", lambda s, m=m, ps=ps: s.activation(
                    arS[:, 512 * m:512 * (m + 1)], ps[:, 0:512],
                    AF.Identity, bias=bT["k"][:, m:m + 1]),
                  reads=[pk, "bk"], writes=[SK[(512 * m) // 1024]], tag=f"qke{blk}_{m}")
                A("gpsimd", lambda g, m=m, blk=blk: g.dma_start(
                    out=kC[128 * m:128 * (m + 1), 512 * blk:512 * (blk + 1)],
                    in_=arS[:, 512 * m:512 * (m + 1)]),
                  reads=[SK[(512 * m) // 1024]], writes=[f"kC{m}"], kind="dma", tag=f"qkd{blk}_{m}")
                A("scalar", lambda s, m=m: s.activation(
                    sqsc[:, :], arS[:, 512 * m:512 * (m + 1)], AF.Square),
                  reads=[SK[(512 * m) // 1024]], writes=["sqsc"], tag=f"ksq{blk}_{m}")
                pdq = 512 * (m % 2)
                pdk_ = f"pD{m % 2}"
                A("tensor", lambda t, pdq=pdq: t.matmul(
                    pD[0:2, pdq:pdq + 512], maskq[:, :], sqsc[:, :], start=True, stop=True),
                    reads=["maskq", "sqsc"], writes=[pdk_], tag=f"ksm{blk}_{m}")
                A("vector", lambda v, pdq=pdq: v.reduce_max(
                    mstats[0:2, 113:114], pD[0:2, pdq:pdq + 512], axis=AX.X),
                  reads=[pdk_], writes=["mredk"], tag=f"ksr{blk}_{m}")
                A("vector", lambda v, m=m: v.tensor_tensor(
                    mstats[0:2, 16 + m:16 + m + 1], mstats[0:2, 16 + m:16 + m + 1],
                    mstats[0:2, 113:114], op=ALU.max),
                  reads=["mredk", "mstats"], writes=["mstats"], tag=f"ksx{blk}_{m}")

        def qkv_v(blk):
            xv, xkeys = qkv_xv(blk)
            xln_views = [xv[:, 520 * j + 1:520 * j + 513] for j in range(NJ)]
            xk2 = [xkeys[(520 * j) // 1024] for j in range(NJ)] if len(xkeys) == 5 else xkeys
            for m in range(NJ):
                ps, pk = proj_mm(16, xln_views, xk2, m, m % 3, f"qv{blk}_{m}")
                A("scalar", lambda s, m=m, ps=ps: s.activation(
                    arS[:, 4096 + 512 * m:4096 + 512 * (m + 1)], ps[:, 0:512],
                    AF.Identity, bias=bT["v"][:, m:m + 1]),
                  reads=[pk, "bv"], writes=[SK[(4096 + 512 * m) // 1024]], tag=f"qve{blk}_{m}")
            for u in range(4):
                cglob = 4 * blk + u
                pb = 1024 * (u % 2)
                pbk = f"pD{u % 2}"
                i0 = None
                for m in range(NJ):
                    i0 = A("tensor", lambda t, m=m, u=u, pb=pb: t.transpose(
                        pDb[:, pb + 128 * m:pb + 128 * (m + 1)],
                        arS[:, 4096 + 512 * m + 128 * u:4096 + 512 * m + 128 * (u + 1)], idb[:, :]),
                        reads=[SK[(4096 + 512 * m) // 1024], "idb"],
                        writes=([pbk] if m == 0 else []), tag=f"vt{blk}_{u}_{m}")
                pr.set_last_writer(pbk, i0)
                vtv, vtk = vt_bufs[u % 2]
                A("vector", lambda v, pb=pb, vtv=vtv: v.tensor_copy(
                    vtv.rearrange("p (h m) -> p h m", m=65)[:, :, 0:64],
                    pDb[:, pb:pb + 1024].rearrange("p (h m) -> p h m", m=64)),
                  reads=[pbk], writes=vtk, tag=f"ve{blk}_{u}")
                for q0 in range(4):
                    A("gpsimd", lambda g, q0=q0, cglob=cglob, vtv=vtv: g.dma_start(
                        out=vC[:, :].rearrange("(h p) m -> p h m", p=128)[:, 4 * q0:4 * (q0 + 1), 65 * cglob:65 * (cglob + 1)],
                        in_=vtv[:, 65 * 4 * q0:65 * 4 * (q0 + 1)].rearrange("p (h m) -> p h m", m=65)),
                      reads=vtk, writes=[f"vC{q0}"], kind="dma", tag=f"vd{blk}_{u}_{q0}")

        phase_stats("qps", tiles=False)
        xv0, xkeys0 = qkv_xv(0)
        xb_apply(0, False, "q0", xv0, xkeys0)
        for blk in range(NBLK):
            qkv_q(blk)
            if blk + 1 < NBLK:
                xvn, xkeysn = qkv_xv(blk + 1)
                xb_apply(blk + 1, False, f"q{blk + 1}", xvn, xkeysn)
            qkv_k(blk)
            qkv_v(blk)

        def cc_kv(hp):
            A("gpsimd", lambda g, hp=hp: g.collective_compute(
                "AllGather", ALU.bypass, replica_groups=PAIRS,
                ins=[kC[128 * hp:128 * (hp + 1), :].opt()],
                outs=[kG[256 * hp:256 * (hp + 1), :].opt()]),
              reads=[f"kC{hp}"], writes=[f"kG{hp}"], kind="cc", tag=f"ccK{hp}")
            A("gpsimd", lambda g, hp=hp: g.collective_compute(
                "AllGather", ALU.bypass, replica_groups=PAIRS,
                ins=[vC[256 * hp:256 * (hp + 1), :].opt()],
                outs=[vG[512 * hp:512 * (hp + 1), :].opt()]),
              reads=[f"vC{hp // 2}"], writes=[f"vG{hp}"], kind="cc", tag=f"ccV{hp}")
        A("gpsimd", lambda g, hp=0: g.collective_compute(
            "AllGather", ALU.bypass, replica_groups=PAIRS,
            ins=[kC[0:128, :].opt()], outs=[kG[0:256, :].opt()]),
          reads=["kC0"], writes=["kG0"], kind="cc", tag="ccK0")
        A("gpsimd", lambda g: g.dma_start(out=mxC[:, :], in_=mstats[0:2, 16:32]),
          reads=["mstats"], writes=["mxC"], kind="dma", tag="mxw")
        A("gpsimd", lambda g: g.collective_compute(
            "AllReduce", ALU.max, replica_groups=PAIRS,
            ins=[mxC[:, :].opt()], outs=[mxR[:, :].opt()]),
          reads=["mxC"], writes=["mxR"], kind="cc", tag="ccM")
        A("sync", lambda e: e.dma_start(out=mstats[0:2, 16:32], in_=mxR[:, :]),
          reads=["mxR", "mstats"], writes=["mstats"], kind="dma", tag="mxr")
        A("gpsimd", lambda g: g.collective_compute(
            "AllGather", ALU.bypass, replica_groups=PAIRS,
            ins=[vC[0:256, :].opt()], outs=[vG[0:512, :].opt()]),
          reads=["vC0"], writes=["vG0"], kind="cc", tag="ccV0")
        for hp in range(1, NJ):
            cc_kv(hp)
        A("vector", lambda v: v.tensor_tensor(mstats[0:2, 32:48], mstats[0:2, 0:16],
                                              mstats[0:2, 16:32], op=ALU.mult),
          reads=["mstats"], writes=["mstats"], tag="mprod")
        A("vector", lambda v: v.tensor_scalar(mstats[0:2, 32:48], mstats[0:2, 32:48],
                                              1e-30, None, op0=ALU.add),
          reads=["mstats"], writes=["mstats"], tag="mg")
        A("scalar", lambda s: s.activation(mstats[0:2, 96:112], mstats[0:2, 32:48], AF.Ln),
          reads=["mstats"], writes=["mstats"], tag="mln")
        A("scalar", lambda s: s.activation(mstats[0:2, 96:112], mstats[0:2, 96:112],
                                           AF.Exp, scale=0.5),
          reads=["mstats"], writes=["mstats"], tag="mex")
        A("vector", lambda v: v.tensor_scalar(mstats[0:2, 96:112], mstats[0:2, 96:112],
                                              0.125, None, op0=ALU.mult),
          reads=["mstats"], writes=["mstats"], tag="msc")
        A("gpsimd", lambda g: g.dma_start(
            out=mrow[0:1, 0:16].rearrange("p (m two) -> p m two", two=2)[:, :, 0:1],
            in_=mstats[0:1, 96:104], allow_slow_non_contiguous=True),
          reads=["mstats"], writes=["mrow"], kind="dma", tag="mrow0")
        A("gpsimd", lambda g: g.dma_start(
            out=mrow[0:1, 0:16].rearrange("p (m two) -> p m two", two=2)[:, :, 1:2],
            in_=mstats[1:2, 96:104], allow_slow_non_contiguous=True),
          reads=["mstats", "mrow"], writes=["mrow"], kind="dma", tag="mrow1")
        A("vector", lambda v: v.tensor_tensor(
            mrow[0:1, 16:24].rearrange("p (h one) -> p h one", one=1),
            mrow[0:1, 0:16].rearrange("p (h two) -> p h two", two=2)[:, :, 0:1],
            mrow[0:1, 0:16].rearrange("p (h two) -> p h two", two=2)[:, :, 1:2],
            op=ALU.max),
          reads=["mrow"], writes=["mrow"], tag="mpair")
        A("tensor", lambda t: t.matmul(pD[:, 0:8], negonesf[0:1, :], mrow[0:1, 16:24],
                                       start=True, stop=True),
          reads=["negonesf", "mrow"], writes=["pD0"], tag="mbc")
        A("vector", lambda v: v.tensor_copy(mne[:, :], pD[:, 0:8]),
          reads=["pD0"], writes=["mne"], tag="mne")

        vhs_off = [8192, 12352]
        av_banks = [(pC, "pCa", "pCb"), (pD, "pD0", "pD1")]
        att_pend = [None]

        def att_flush():
            if att_pend[0] is not None:
                avfn, epifn = att_pend[0]
                att_pend[0] = None
                avfn()
                if epifn is not None:
                    epifn()

        def make_epilogue(hp, s):
            pV, ka, kb = av_banks[s % 2]

            def epi():
                A("vector", lambda v, hp=hp, s=s, pV=pV: v.tensor_copy(
                    arS[0:64, 2048 * hp + 512 * s:2048 * hp + 512 * (s + 1)],
                    pV[0:64, 0:512]),
                  reads=[ka], writes=[SK[2 * hp], SK[2 * hp + 1]],
                  tag=f"ameA{hp}_{s}")
                A("vector", lambda v, hp=hp, s=s, pV=pV: v.tensor_copy(
                    arS[64:128, 2048 * hp + 512 * s:2048 * hp + 512 * (s + 1)],
                    pV[0:64, 512:1024]),
                  reads=[kb], writes=[SK[2 * hp], SK[2 * hp + 1]],
                  tag=f"ameB{hp}_{s}")
                A("vector", lambda v, pV=pV: v.tensor_copy(arWf[64:65, 10240:10752], pV[64:65, 0:512]),
                  reads=[ka], writes=wkeys(20480, 21504), tag=f"ad1{hp}_{s}")
                A("vector", lambda v, pV=pV: v.tensor_copy(arWf[64:65, 10752:11264], pV[64:65, 512:1024]),
                  reads=[kb, *wkeys(20480, 21504)], writes=wkeys(21504, 22528), tag=f"ad2{hp}_{s}")
                A("gpsimd", lambda g_, hp=hp, s=s: g_.dma_start(
                    out=arSf[2 * hp:2 * hp + 1, 8192 + 512 * s:8192 + 512 * (s + 1)],
                    in_=arWf[64:65, 10240:10752]),
                  reads=wkeys(20480, 21504), writes=[f"den{2 * hp}_{s}"],
                  kind="dma", tag=f"admA{hp}_{s}")
                A("gpsimd", lambda g_, hp=hp, s=s: g_.dma_start(
                    out=arSf[2 * hp + 1:2 * hp + 2, 8192 + 512 * s:8192 + 512 * (s + 1)],
                    in_=arWf[64:65, 10752:11264]),
                  reads=wkeys(21504, 22528), writes=[f"den{2 * hp + 1}_{s}"],
                  kind="dma", tag=f"admB{hp}_{s}")
            return epi

        for hp in range(NJ):
            p2 = hp % 2
            kc = 4096 * p2
            kkey = f"Khs{p2}"
            vc = vhs_off[p2]
            vkey = f"Vh{p2}"
            for r in range(2):
                A("sync", lambda e, hp=hp, r=r, kc=kc: e.dma_start(
                    out=arW[:, kc + 2048 * r:kc + 2048 * (r + 1)],
                    in_=kG[256 * hp + 128 * r:256 * hp + 128 * (r + 1), :]),
                  reads=[f"kG{hp}"], writes=[kkey] + wkeys(kc + 2048 * r, kc + 2048 * (r + 1)),
                  kind="dma", tag=f"akl{hp}_{r}")
                for tt in range(2):
                    A("sync", lambda e, hp=hp, r=r, tt=tt, vc=vc: e.dma_start(
                        out=arW[:, vc + 2080 * tt + 1040 * r:vc + 2080 * tt + 1040 * (r + 1)],
                        in_=vG[512 * hp + 256 * r + 128 * tt:512 * hp + 256 * r + 128 * (tt + 1), :]),
                      reads=[f"vG{hp}"],
                      writes=[vkey] + wkeys(vc + 2080 * tt + 1040 * r, vc + 2080 * tt + 1040 * (r + 1)),
                      kind="dma", tag=f"avl{hp}_{r}_{tt}")
            for s in range(4):
                pV, ka, kb = av_banks[s % 2]

                def emit_av(g, pt, ptk, hp=hp, s=s, vc=vc, vkey=vkey, pV=pV, ka=ka, kb=kb):
                    last = None
                    for tt in range(2):
                        vcol = vc + 2080 * tt + 1040 * (g // 16) + 65 * (g % 16)
                        last = A("tensor", lambda t, g=g, tt=tt, vcol=vcol, pt=pt, pV=pV: t.matmul(
                            pV[0:65, 512 * tt:512 * (tt + 1)],
                            arW[:, vcol:vcol + 65], pt[:, 512 * tt:512 * (tt + 1)],
                            start=(g == 0), stop=(g == 31)),
                            reads=[vkey, ptk] + ([ka, kb] if (g, tt) == (0, 0) else []),
                            writes=([ka, kb] if (g, tt) == (0, 0) else []),
                            tag=f"av{hp}_{s}_{g}_{tt}")
                    if g == 31:
                        pr.set_last_writer(ka, last)
                        pr.set_last_writer(kb, last)

                for g in range(32):
                    pS = pA if g % 2 == 0 else pB
                    pSk = "pA" if g % 2 == 0 else "pB"
                    pt = arW[:, 18432 + 1024 * (g % 2):18432 + 1024 * (g % 2 + 1)]
                    ptk = WK[18 + g % 2]
                    last = None
                    for tt in range(2):
                        rb = 64 * tt
                        last = A("tensor", lambda t, g=g, tt=tt, s=s, pS=pS, rb=rb, hp=hp, kc=kc: t.matmul(
                            pS[:, 512 * tt:512 * (tt + 1)],
                            arW[rb:rb + 64, kc + 128 * g:kc + 128 * (g + 1)],
                            arQ[rb:rb + 64, 2048 * hp + 512 * s:2048 * hp + 512 * (s + 1)],
                            start=True, stop=True),
                            reads=[kkey, f"Q{hp}"], writes=([pSk] if tt == 0 else []),
                            tag=f"as{hp}_{s}_{g}_{tt}")
                    pr.set_last_writer(pSk, last)
                    A("scalar", lambda sc, pS=pS, pt=pt, hp=hp: sc.activation(
                        pt, pS[:, :], AF.Exp, bias=mne[:, hp:hp + 1], scale=0.125),
                      reads=[pSk, "mne"], writes=[ptk], tag=f"ae{hp}_{s}_{g}")
                    att_flush()
                    att_pend[0] = ((lambda g=g, pt=pt, ptk=ptk, f=emit_av: f(g, pt, ptk)),
                                   (make_epilogue(hp, s) if g == 31 else None))
        att_flush()

        wo_views = [arQ[:, 8192 + 1024 * j:8192 + 1024 * (j + 1)] for j in range(6)] + \
                   [arX[:, 0:1024], arX[:, 1024:2048]]
        wo_keys = [[f"Q{4 + j // 2}"] for j in range(6)] + \
                  [[f"X{j}" for j in range(NJ)]] * 2
        for j in range(NJ):
            stg = arQf[:, 2048 + 1024 * (j % 2):2048 + 1024 * (j % 2 + 1)]
            sks = [f"Q{2 + j % 2}"]
            A("sync", lambda e, j=j, stg=stg: e.dma_start(
                out=stg, in_=ct["o"][128 * j:128 * (j + 1), :]),
              writes=sks, kind="dma", tag=f"wold{j}")
            A("vector", lambda v, j=j, stg=stg: v.tensor_copy(wo_views[j][:, :], stg),
              reads=sks, writes=wo_keys[j], tag=f"wocv{j}")

        A("vector", lambda v: v.tensor_copy(oddm[:, :], oddm[:, :]),
          writes=WK + ["Khs0", "Khs1", "Vh0", "Vh1"], reads=["oddm"], tag="fence2")

        den_keys = [f"den{h}_{s}" for h in range(NH) for s in range(4)]
        A("vector", lambda v: v.reciprocal(arSf[0:16, 8192:10240], arSf[0:16, 8192:10240]),
          reads=den_keys, writes=["rden"], tag="recip")
        for j in range(NJ):
            bb = 2048 * (j % 2)
            bk = ["Q0", "Q1"] if j % 2 == 0 else ["Q2", "Q3"]
            A("gpsimd", lambda g, j=j, bb=bb: g.dma_start(
                out=arQf[0:2, bb:bb + 2048],
                in_=arSf[2 * j:2 * (j + 1), 8192:10240]),
              reads=["rden"], writes=bk, kind="dma", tag=f"rc{j}")
            for half in range(2):
                for u in range(2):
                    A("tensor", lambda t, u=u, half=half, bb=bb: t.matmul(
                        pD[:, 512 * u:512 * (u + 1)], pairmask[:, :],
                        arQf[0:2, bb + 1024 * half + 512 * u:bb + 1024 * half + 512 * (u + 1)],
                        start=True, stop=True),
                        reads=["pairmask"] + bk,
                        writes=[f"pD{u}"], tag=f"rb{j}_{half}_{u}")
                A("vector", lambda v, j=j, half=half: v.tensor_tensor(
                    arS[:, 2048 * j + 1024 * half:2048 * j + 1024 * (half + 1)],
                    arS[:, 2048 * j + 1024 * half:2048 * j + 1024 * (half + 1)],
                    pD[:, 0:1024], op=ALU.mult),
                  reads=["pD0", "pD1", SK[2 * j], SK[2 * j + 1]],
                  writes=[SK[2 * j], SK[2 * j + 1]], tag=f"rs{j}_{half}")

        merge_views = [arS[:, 2048 * j:2048 * (j + 1)] for j in range(NJ)]
        for blk in (3, 0, 1, 2):
            mv = [merge_views[j][:, 512 * blk:512 * (blk + 1)] for j in range(NJ)]
            mk = [SK[(2048 * j + 512 * blk) // 1024] for j in range(NJ)]
            for m in range(NJ):
                ps, pk = proj_mm(0, mv, mk, m, m % 3, f"wo{blk}_{m}",
                                 w_views=wo_views, w_keys=wo_keys)
                A("scalar", lambda s, m=m, ps=ps: s.activation(
                    arS[:, 16384 + 512 * m:16384 + 512 * (m + 1)], ps[:, 0:512],
                    AF.Identity, bias=bT["o"][:, m:m + 1]),
                  reads=[pk, "bo"],
                  writes=[SK[(16384 + 512 * m) // 1024], "rden"],
                  tag=f"woe{blk}_{m}")
            for u in range(4):
                ti = 4 * blk + u
                pb = 1024 * (u % 2)
                pbk = f"pD{u % 2}"
                i0 = None
                for m in range(NJ):
                    i0 = A("tensor", lambda t, m=m, u=u, pb=pb: t.transpose(
                        pDb[:, pb + 128 * m:pb + 128 * (m + 1)],
                        arS[:, 16384 + 512 * m + 128 * u:16384 + 512 * m + 128 * (u + 1)], idb[:, :]),
                        reads=[SK[(16384 + 512 * m) // 1024], "idb"],
                        writes=([pbk] if m == 0 else []), tag=f"wot{blk}_{u}_{m}")
                pr.set_last_writer(pbk, i0)
                A("vector", lambda v, ti=ti, pb=pb: v.tensor_tensor(
                    resid[ti][:, :], resid[ti][:, :], pDb[:, pb:pb + 1024], op=ALU.add),
                  reads=[pbk, f"res{ti}"], writes=[f"res{ti}"], tag=f"woa{blk}_{u}")
                ln_stats(ti, u % 2, f"f2pre{ti}")
            if blk == 3:
                A("gpsimd", lambda g: g.dma_start(out=bC[:, :], in_=resid[15][127:128, :]),
                  reads=["res15"], writes=["bC"], kind="dma", tag="bcw")
                A("gpsimd", lambda g: g.collective_compute(
                    "AllGather", ALU.bypass, replica_groups=PAIRS,
                    ins=[bC[:, :].opt()], outs=[bG[:, :].opt()]),
                  reads=["bC"], writes=["bG"], kind="cc", tag="ccB")
                A("sync", lambda e: e.dma_start(out=arWf[0:1, 4096:5120], in_=bG[0:1, :]),
                  reads=["bG"], writes=wkeys(8192, 10240), kind="dma", tag="bgr")

        A("vector", lambda v: v.tensor_scalar(resb, arWf[0:1, 4096:5120],
                                              oddm[0:1, 0:1], None, op0=ALU.mult),
          reads=[*wkeys(8192, 10240), "oddm"], writes=["resb", SK[14], SK[15]], tag="bmask")

        ffn_phase("f2", pre_stats=True, from_cache=True)

        for i in range(NTL):
            A("sync", lambda e, i=i: e.dma_start(out=y_out[128 * i:128 * (i + 1), :], in_=resid[i][:, :]),
              reads=[f"res{i}"], kind="dma", tag=f"st{i}")

        pr.emit(ex)
    return nc


_NC_CACHE = []


def kernel(L0_attn_x, L0_attn_wq_c, L0_attn_wq_b, L0_attn_wk_c, L0_attn_wk_b,
           L0_attn_wv_c, L0_attn_wv_b, L0_attn_wo_c, L0_attn_wo_b,
           L0_ffn_W_r, L0_ffn_W_k, L0_ffn_W_v, L0_ffn_mu_r, L0_ffn_mu_k):
    x = np.asarray(L0_attn_x, dtype=np.float32)
    B, TT, DD = x.shape
    assert (B, TT, DD) == (4, 4096, 1024)

    idx = (np.arange(D)[None, :] - np.arange(D)[:, None]) % D
    def circ(wc):
        return np.ascontiguousarray(np.asarray(wc, np.float32)[idx])
    def fm8(v):
        return np.ascontiguousarray(np.asarray(v, np.float32).reshape(NJ, 128).T)

    common = {
        "idf": np.eye(128, dtype=np.float32),
        "ct_q": circ(L0_attn_wq_c), "ct_k": circ(L0_attn_wk_c),
        "ct_v": circ(L0_attn_wv_c), "ct_o": circ(L0_attn_wo_c),
        "b_q": fm8(L0_attn_wq_b), "b_k": fm8(L0_attn_wk_b),
        "b_v": fm8(L0_attn_wv_b), "b_o": fm8(L0_attn_wo_b),
        "mu_r": fm8(L0_ffn_mu_r), "mu_k": fm8(L0_ffn_mu_k),
        "wrT": np.ascontiguousarray(np.asarray(L0_ffn_W_r, np.float32).T),
        "wkT": np.ascontiguousarray(np.asarray(L0_ffn_W_k, np.float32).T),
        "wvT": np.ascontiguousarray(np.asarray(L0_ffn_W_v, np.float32).T),
    }
    zrow = np.zeros((1, D), np.float32)
    in_maps = []
    for c in range(8):
        seq, half = c // 2, c % 2
        t0 = half * T
        m = dict(common)
        m["x"] = np.ascontiguousarray(x[seq, t0:t0 + T, :])
        m["xbnd"] = np.ascontiguousarray(x[seq, t0 - 1:t0, :]) if half else zrow
        m["oddmask"] = np.array([[float(half)]], np.float32)
        in_maps.append(m)

    if not _NC_CACHE:
        _NC_CACHE.append(build_nc())
    nc = _NC_CACHE[0]
    res = run_bass_kernel_spmd(nc, in_maps, core_ids=list(range(8)))
    y = np.empty((4, 4096, 1024), np.float32)
    for c in range(8):
        seq, half = c // 2, c % 2
        y[seq, half * T:(half + 1) * T, :] = np.asarray(res.results[c]["y"])
    return y


# version 36 (final)
# speedup vs baseline: 1.0614x; 1.0446x over previous
import contextlib
import numpy as np
import concourse.bass as bass
import concourse.mybir as mybir
from concourse.bass_utils import run_bass_kernel_spmd

F32 = mybir.dt.float32
BF16 = mybir.dt.bfloat16
AF = mybir.ActivationFunctionType
ALU = mybir.AluOpType
AX = mybir.AxisListType

D = 1024
T = 2048
NH = 16
DH = 64
NJ = 8
NTL = 16
NBLK = 4
BLK = 512
EPS = 1e-5
PAIRS = [[0, 1], [2, 3], [4, 5], [6, 7]]

ENGINES = ("sync", "vector", "scalar", "tensor", "gpsimd")
N_DMA_LANES = 8


class _Op:
    __slots__ = ("idx", "eng", "fn", "deps", "kind", "sem", "val", "needs_inc", "tag")

    def __init__(self, idx, eng, fn, deps, kind, tag):
        self.idx = idx
        self.eng = eng
        self.fn = fn
        self.deps = deps
        self.kind = kind
        self.sem = None
        self.val = None
        self.needs_inc = False
        self.tag = tag


class Prog:
    def __init__(self, nc):
        self.nc = nc
        self.ops = []
        self.last_w = {}
        self.readers = {}

    def add(self, eng, fn, reads=(), writes=(), kind="compute", tag=""):
        deps = set()
        for k in reads:
            i = self.last_w.get(k)
            if i is not None:
                deps.add(i)
        for k in writes:
            i = self.last_w.get(k)
            if i is not None:
                deps.add(i)
            for r in self.readers.get(k, ()):
                deps.add(r)
        idx = len(self.ops)
        self.ops.append(_Op(idx, eng, fn, deps, kind, tag))
        for k in reads:
            self.readers.setdefault(k, []).append(idx)
        for k in writes:
            self.last_w[k] = idx
            self.readers[k] = []
        return idx

    def set_last_writer(self, key, idx):
        self.last_w[key] = idx
        self.readers[key] = []

    def emit(self, ex):
        nc = self.nc
        ops = self.ops
        for op in ops:
            tdeps = [d for d in op.deps if ops[d].eng == "tensor"]
            if len(tdeps) > 1:
                keep = max(tdeps)
                for d in tdeps:
                    if d != keep:
                        op.deps.discard(d)
        for op in ops:
            for d in op.deps:
                ops[d].needs_inc = True
        for op in ops:
            if op.kind in ("dma", "cc"):
                op.needs_inc = True

        tick = {}
        dma_lanes = {}
        any_compute = {e: any(o.eng == e and o.kind == "compute" and o.needs_inc
                              for o in ops) for e in ENGINES}
        n_dma = {e: sum(1 for o in ops if o.eng == e and o.kind == "dma")
                 for e in ENGINES}
        for e in ENGINES:
            if any_compute[e]:
                tick[e] = ex.enter_context(nc.semaphore(f"tick_{e}"))
            dma_lanes[e] = [ex.enter_context(nc.semaphore(f"dl_{e}{i}"))
                            for i in range(min(N_DMA_LANES, n_dma[e]))]
        n_cc = sum(1 for o in ops if o.kind == "cc")
        cc_sems = [ex.enter_context(nc.semaphore(f"cc{i}")) for i in range(n_cc)]

        cnt = {}
        dma_i = {e: 0 for e in ENGINES}
        cc_cnt = 0
        lane_prev = {}
        for op in ops:
            if op.kind == "compute":
                if op.needs_inc:
                    s = tick[op.eng]
                    cnt[s.num] = cnt.get(s.num, 0) + 1
                    op.sem, op.val = s, cnt[s.num]
            elif op.kind == "dma":
                lanes = dma_lanes[op.eng]
                s = lanes[dma_i[op.eng] % len(lanes)]
                dma_i[op.eng] += 1
                cnt[s.num] = cnt.get(s.num, 0) + 16
                op.sem, op.val = s, cnt[s.num]
                if s.num in lane_prev:
                    op.deps.add(lane_prev[s.num])
                lane_prev[s.num] = op.idx
            elif op.kind == "cc":
                op.sem, op.val = cc_sems[cc_cnt], 1
                cc_cnt += 1

        snapshots = {}
        know = {e: {} for e in ENGINES}
        waits_for_op = [None] * len(ops)
        for op in ops:
            k = know[op.eng]
            w = []
            for d in sorted(op.deps):
                dop = ops[d]
                if dop.sem is None:
                    raise RuntimeError(f"dep of {op.tag or op.idx} on "
                                       f"{dop.tag or d} has no sem")
                if k.get(dop.sem.num, 0) >= dop.val:
                    continue
                w.append((dop.sem, dop.val))
                k[dop.sem.num] = dop.val
                snap = snapshots.get((dop.sem.num, dop.val))
                if snap:
                    for s2, v2 in snap.items():
                        if k.get(s2, 0) < v2:
                            k[s2] = v2
            waits_for_op[op.idx] = w
            if op.needs_inc:
                snap = dict(k)
                snap[op.sem.num] = op.val
                snapshots[(op.sem.num, op.val)] = snap

        by_eng = {e: [o for o in ops if o.eng == e] for e in ENGINES}
        final = dict(cnt)
        for s in cc_sems:
            final[s.num] = 1
        sem_by_id = {}
        for e in ENGINES:
            if e in tick:
                sem_by_id[tick[e].num] = tick[e]
            for s in dma_lanes[e]:
                sem_by_id[s.num] = s
        for s in cc_sems:
            sem_by_id[s.num] = s

        block = ex.enter_context(nc.Block())

        def run_engine(engobj, eng):
            for op in by_eng[eng]:
                for sem, val in waits_for_op[op.idx]:
                    engobj.wait_ge(sem, val)
                inst = op.fn(engobj)
                if op.needs_inc:
                    if op.kind == "dma":
                        inst.then_inc(op.sem, 16)
                    elif op.kind == "cc":
                        inst.then_inc(op.sem)
                    else:
                        inst.then_inc(op.sem, 1)

        @block.sync
        def _(eng):
            run_engine(eng, "sync")
            for sid, v in sorted(final.items()):
                eng.wait_ge(sem_by_id[sid], v)

        @block.vector
        def _(eng):
            run_engine(eng, "vector")

        @block.scalar
        def _(eng):
            run_engine(eng, "scalar")

        @block.tensor
        def _(eng):
            run_engine(eng, "tensor")

        @block.gpsimd
        def _(eng):
            run_engine(eng, "gpsimd")


def build_nc():
    nc = bass.Bass("TRN2", target_bir_lowering=False)

    x_in = nc.dram_tensor("x", [T, D], F32, kind="ExternalInput").ap()
    xbnd = nc.dram_tensor("xbnd", [1, D], F32, kind="ExternalInput").ap()
    oddm_in = nc.dram_tensor("oddmask", [1, 1], F32, kind="ExternalInput").ap()
    idf = nc.dram_tensor("idf", [128, 128], F32, kind="ExternalInput").ap()
    ct = {w: nc.dram_tensor(f"ct_{w}", [D, D], F32, kind="ExternalInput").ap()
          for w in ("q", "k", "v", "o")}
    bvec = {w: nc.dram_tensor(f"b_{w}", [128, NJ], F32, kind="ExternalInput").ap()
            for w in ("q", "k", "v", "o")}
    mu_in = {w: nc.dram_tensor(f"mu_{w}", [128, NJ], F32, kind="ExternalInput").ap()
             for w in ("r", "k")}
    wT = {w: nc.dram_tensor(f"w{w}T", [D, D], F32, kind="ExternalInput").ap()
          for w in ("r", "k", "v")}
    y_out = nc.dram_tensor("y", [T, D], F32, kind="ExternalOutput").ap()

    kC = nc.dram_tensor("kC", [D, T], BF16).ap()
    kG = nc.dram_tensor("kG", [2 * D, T], BF16).ap()
    vC = nc.dram_tensor("vC", [NH * 128, 1040], BF16).ap()
    vG = nc.dram_tensor("vG", [2 * NH * 128, 1040], BF16).ap()
    bC = nc.dram_tensor("bC", [1, D], F32).ap()
    mxC = nc.dram_tensor("mxC", [2, 16], F32).ap()
    mxR = nc.dram_tensor("mxR", [2, 16], F32).ap()
    bG = nc.dram_tensor("bG", [2, D], F32).ap()
    wbf = nc.dram_tensor("wbf", [128, 24576], BF16).ap()

    ex = contextlib.ExitStack()
    with ex:
        def S(name, shape, dt=F32):
            return ex.enter_context(nc.sbuf_tensor(name, shape, dt))
        def P(name, shape, dt=F32):
            return ex.enter_context(nc.psum_tensor(name, shape, dt))

        resid = [S(f"res{i}", [128, D]) for i in range(NTL)]
        arW = S("arW", [128, 24576], BF16)
        arS = S("arS", [128, 20480], BF16)
        arQ = S("arQ", [128, 16384], BF16)
        arX = S("arX", [128, 4160], BF16)

        bn12 = [S(f"bn12_{i}", [128, 12]) for i in range(2)]
        stb = S("stb", [128, 128])
        bnb12 = S("bnb12", [1, 12])
        sttb = S("sttb", [1, 32])

        muT = {w: S(f"muT_{w}", [128, NJ]) for w in ("r", "k")}
        bT = {w: S(f"bT_{w}", [128, NJ]) for w in ("q", "k", "v", "o")}
        idb = S("idb", [128, 128], BF16)
        onesb1 = S("onesb1", [1, 1], BF16)
        negonesf = S("negonesf", [1, 128])
        onesf = S("onesf", [1, 128])
        maskq = S("maskq", [128, 2], BF16)
        pairmask = S("pairmask", [2, 128])
        mrow = S("mrow", [1, 24])
        mstats = S("mstats", [2, 128])
        sqsc = S("sqsc", [128, 512], BF16)
        mne = S("mne", [128, 8])
        oddm = S("oddm", [1, 1])
        halfc = S("halfc", [128, 1])
        xtm = [S(f"xtm{i}", [128, D], BF16) for i in range(2)]

        arWf = arW[:, :].bitcast(F32)
        arSf = arS[:, :].bitcast(F32)
        arQf = arQ[:, :].bitcast(F32)
        resb = arSf[32:33, 7168:8192]
        xlnb = arS[0:1, 14336:15360]
        convst = [arQf[:, 1024 * i:1024 * (i + 1)] for i in range(2)]

        pA = P("pA", [128, 1024])
        pB = P("pB", [128, 1024])
        pC = P("pC", [128, 1024])
        pD = P("pD", [128, 1024])
        pDb = pD[:, :].bitcast(BF16)

        pr = Prog(nc)
        A = pr.add

        def wslot(i, n=1):
            return arW[:, 1024 * i:1024 * (i + 1 + (n - 1))]
        WK = [f"W{i}" for i in range(24)]
        SK = [f"S{i}" for i in range(20)]

        def wkeys(c0, c1):
            return [WK[i] for i in range(c0 // 1024, (c1 + 1023) // 1024)]
        def skeys(c0, c1):
            return [SK[i] for i in range(c0 // 1024, (c1 + 1023) // 1024)]

        for i in range(4):
            A("sync", lambda e, i=i: e.dma_start(out=resid[i][:, :], in_=x_in[128 * i:128 * (i + 1), :]),
              writes=[f"res{i}"], kind="dma")
        A("sync", lambda e: e.dma_start(out=resb, in_=xbnd), writes=["resb"], kind="dma")

        A("sync", lambda e: e.dma_start(out=convst[0][:, 0:128], in_=idf), writes=["Q0"], kind="dma")
        A("vector", lambda v: v.tensor_copy(idb[:, :], convst[0][:, 0:128]), reads=["Q0"], writes=["idb"])
        A("vector", lambda v: v.memset(onesb1[:, :], 1.0), writes=["onesb1"])
        A("vector", lambda v: v.memset(halfc[:, :], 0.5), writes=["halfc"])
        A("vector", lambda v: v.memset(onesf[:, :], 1.0), writes=["onesf"])
        A("vector", lambda v: v.memset(negonesf[:, :], -1.0), writes=["negonesf"])
        A("vector", lambda v: v.memset(maskq[:, :], 0.0), writes=["maskq"])
        A("vector", lambda v: v.memset(maskq[0:64, 0:1], 1.0), writes=["maskq"])
        A("vector", lambda v: v.memset(maskq[64:128, 1:2], 1.0), writes=["maskq"])
        A("vector", lambda v: v.memset(pairmask[:, :], 0.0), writes=["pairmask"])
        A("vector", lambda v: v.memset(pairmask[0:1, 0:64], 1.0), writes=["pairmask"])
        A("gpsimd", lambda g: g.dma_start(out=pairmask[1:2, 64:128], in_=onesf[0:1, 0:64]),
          reads=["onesf", "pairmask"], writes=["pairmask"], kind="dma")
        for w in ("r", "k"):
            A("sync", lambda e, w=w: e.dma_start(out=muT[w][:, :], in_=mu_in[w]), writes=[f"mu{w}"], kind="dma")
        for w in ("q", "k", "v", "o"):
            A("sync", lambda e, w=w: e.dma_start(out=bT[w][:, :], in_=bvec[w]),
              writes=[f"b{w}"], kind="dma")
        A("sync", lambda e: e.dma_start(out=oddm[:, :], in_=oddm_in), writes=["oddm"], kind="dma")

        conv_n = [0]

        def load_weight_chunks(dram_ap, slot0, tag):
            for j in range(NJ):
                ci = conv_n[0] % 2
                conv_n[0] += 1
                A("sync", lambda e, j=j, ci=ci, d=dram_ap: e.dma_start(
                    out=convst[ci], in_=d[128 * j:128 * (j + 1), :]),
                  writes=[f"Q{ci}"], kind="dma", tag=f"{tag}ld{j}")
                A("scalar", lambda v, j=j, ci=ci, s=slot0: v.copy(
                    wslot(s + j)[:, :], convst[ci]),
                  reads=[f"Q{ci}"], writes=[WK[slot0 + j]], kind="compute", tag=f"{tag}cv{j}")

        INT32 = mybir.dt.int32

        def ln_stats(ti, parity, tag):
            rk = f"res{ti}"
            b = bn12[parity]
            bk = f"bn{parity}"
            A("vector", lambda v: v.bn_stats(b[:, 0:6], resid[ti][:, 0:512]),
              reads=[rk], writes=[bk], tag=f"{tag}bnA")
            A("vector", lambda v: v.bn_stats(b[:, 6:12], resid[ti][:, 512:1024]),
              reads=[rk, bk], writes=[bk], tag=f"{tag}bnB")
            A("vector", lambda v, ti=ti: v.bn_aggr(stb[:, 8 * ti:8 * ti + 2], b[:, :]),
              reads=[bk], writes=[f"stbT{ti}"], tag=f"{tag}ag")

        def phase_stats(tag, tiles=True):
            if tiles:
                for ti in range(NTL):
                    ln_stats(ti, ti % 2, f"{tag}st{ti}")
            s3 = stb[:, :].rearrange("p (t c) -> p t c", c=8)
            def col(b, c):
                return s3[:, 4 * b:4 * (b + 1), c:c + 1].squeeze(2)
            for b in range(4):
                tk = [f"stbT{4 * b + u}" for u in range(4)]
                fk = f"stbF{b}"
                A("vector", lambda v, b=b: v.tensor_scalar(col(b, 2), col(b, 1), EPS, None, op0=ALU.add),
                  reads=tk, writes=[fk], tag=f"{tag}ve{b}")
                A("scalar", lambda s, b=b: s.activation(col(b, 3), col(b, 2), AF.Ln),
                  reads=[fk], writes=[fk], tag=f"{tag}ln{b}")
                A("scalar", lambda s, b=b: s.activation(col(b, 3), col(b, 3), AF.Exp, scale=-0.5),
                  reads=[fk], writes=[fk], tag=f"{tag}ex{b}")

        def ln_bnd(tag):
            A("vector", lambda v: v.bn_stats(bnb12[:, 0:6], resb[0:1, 0:512]),
              reads=["resb"], writes=["bnb"], tag=f"{tag}bnA")
            A("vector", lambda v: v.bn_stats(bnb12[:, 6:12], resb[0:1, 512:1024]),
              reads=["resb", "bnb"], writes=["bnb"], tag=f"{tag}bnB")
            A("vector", lambda v: v.bn_aggr(sttb[:, 0:2], bnb12[:, :]),
              reads=["bnb"], writes=["sttb"], tag=f"{tag}ag")
            A("vector", lambda v: v.tensor_scalar(sttb[:, 2:3], sttb[:, 1:2], EPS, None, op0=ALU.add),
              reads=["sttb"], writes=["sttb"], tag=f"{tag}ve")
            A("scalar", lambda s: s.activation(sttb[:, 3:4], sttb[:, 2:3], AF.Ln),
              reads=["sttb"], writes=["sttb"], tag=f"{tag}l")
            A("scalar", lambda s: s.activation(sttb[:, 3:4], sttb[:, 3:4], AF.Exp, scale=-0.5),
              reads=["sttb"], writes=["sttb"], tag=f"{tag}e")
            A("vector", lambda v: v.tensor_scalar(xlnb, resb,
                                                  sttb[:, 0:1], sttb[:, 3:4],
                                                  op0=ALU.subtract, op1=ALU.mult),
              reads=["resb", "sttb"], writes=[SK[14]], tag=f"{tag}ap")
            for j in range(NJ):
                i0 = A("tensor", lambda t, j=j: t.matmul(pD[:, j:j + 1],
                                                         xlnb[0:1, 128 * j:128 * (j + 1)],
                                                         onesb1[0:1, 0:1],
                                                         start=True, stop=True),
                       reads=[SK[14], "onesb1"], writes=[] if j else ["pD0"], tag=f"{tag}mm{j}")
            pr.set_last_writer("pD0", i0)
            A("vector", lambda v: v.tensor_copy(
                arX[:, 0:4160].rearrange("p (j c) -> p j c", c=520)[:, :, 0:1],
                pD[:, 0:8].rearrange("p (j one) -> p j one", one=1)),
              reads=["pD0"], writes=[f"X{j}" for j in range(NJ)], tag=f"{tag}ev")

        def xb_apply(blk, with_bnd, tag, xv, xkeys):
            if blk > 0 and with_bnd:
                A("vector", lambda v: v.tensor_copy(
                    xv.rearrange("p (j c) -> p j c", c=520)[:, :, 0:1],
                    xv.rearrange("p (j c) -> p j c", c=520)[:, :, 512:513]),
                  reads=xkeys, writes=xkeys, tag=f"{tag}bc")
            for u in range(4):
                ti = 4 * blk + u
                par = u % 2
                A("vector", lambda v, ti=ti, par=par: v.tensor_scalar(
                    xtm[par][:, :], resid[ti][:, :], stb[:, 8 * ti:8 * ti + 1],
                    stb[:, 8 * ti + 3:8 * ti + 4], op0=ALU.subtract, op1=ALU.mult),
                  reads=[f"res{ti}", f"stbF{blk}"], writes=[f"xtm{par}"], tag=f"{tag}ap{u}")
                pb = 1024 * (u % 2)
                pbk = f"pD{u % 2}"
                i0 = None
                for j in range(NJ):
                    ii = A("tensor", lambda t, j=j, par=par, pb=pb: t.transpose(
                        pDb[:, pb + 128 * j:pb + 128 * (j + 1)], xtm[par][:, 128 * j:128 * (j + 1)], idb[:, :]),
                        reads=[f"xtm{par}", "idb"], writes=([pbk] if j == 0 else []), tag=f"{tag}tr{u}_{j}")
                    i0 = ii
                pr.set_last_writer(pbk, i0)
                A("scalar", lambda v, u=u, pb=pb, xv=xv: v.copy(
                    xv.rearrange("p (j c) -> p j c", c=520)[:, :, 1 + 128 * u:1 + 128 * (u + 1)],
                    pDb[:, pb:pb + 1024].rearrange("p (j c) -> p j c", c=128)),
                  reads=[pbk], writes=xkeys, tag=f"{tag}te{u}")



        psum_pool = [pA, pB, pC]
        psum_keys = ["pA", "pB", "pC"]

        def proj_mm(wslot0, rhs_views, rhs_keys, m, pi, tag, w_views=None, w_keys=None):
            ps = psum_pool[pi]
            pk = psum_keys[pi]
            last = None
            for j in range(NJ):
                wv = w_views[j] if w_views is not None else wslot(wslot0 + j)
                wk = w_keys[j] if w_keys is not None else [WK[wslot0 + j]]
                last = A("tensor", lambda t, j=j, m=m, ps=ps, wv=wv: t.matmul(
                    ps[:, 0:512], wv[:, 128 * m:128 * (m + 1)],
                    rhs_views[j], start=(j == 0), stop=(j == NJ - 1)),
                    reads=wk + [rhs_keys[j]],
                    writes=([pk] if j == 0 else []), tag=f"{tag}m{j}")
            pr.set_last_writer(pk, last)
            return ps, pk

        def ffn_phase(tag, pre_stats=False, from_cache=False):
            if tag == "f1":
                for i in range(4, NTL):
                    A("sync", lambda e, i=i: e.dma_start(
                        out=resid[i][:, :], in_=x_in[128 * i:128 * (i + 1), :]),
                      writes=[f"res{i}"], kind="dma")
            if from_cache:
                for i in range(24):
                    A("sync", lambda e, i=i: e.dma_start(
                        out=wslot(i)[:, :], in_=wbf[:, 1024 * i:1024 * (i + 1)]),
                      reads=["wbf"], writes=[WK[i]], kind="dma", tag=f"{tag}wc{i}")
            else:
                load_weight_chunks(wT["r"], 0, f"{tag}wr")
                load_weight_chunks(wT["k"], 8, f"{tag}wk")
                load_weight_chunks(wT["v"], 16, f"{tag}wv")
            ln_bnd(f"{tag}lb")

            def bufs(blk):
                if blk % 2 == 0:
                    xr_v = [arS[:, 4096 + 512 * j:4096 + 512 * (j + 1)] for j in range(NJ)]
                    xr_k = [SK[(4096 + 512 * j) // 1024] for j in range(NJ)]
                    xk_v = [arS[:, 8192 + 512 * j:8192 + 512 * (j + 1)] for j in range(NJ)]
                    xk_k = [SK[(8192 + 512 * j) // 1024] for j in range(NJ)]
                else:
                    xr_v = [arQ[:, 4096 + 512 * j:4096 + 512 * (j + 1)] for j in range(NJ)]
                    xr_k = [f"Q{2 + (512 * j) // 2048}" for j in range(NJ)]
                    xk_v = [arQ[:, 8192 + 512 * j:8192 + 512 * (j + 1)] for j in range(NJ)]
                    xk_k = [f"Q{4 + (512 * j) // 2048}" for j in range(NJ)]
                return xr_v, xr_k, xk_v, xk_k

            out_v = [arQ[:, 12288 + 512 * j:12288 + 512 * (j + 1)] for j in range(NJ)]
            out_k = [f"Q{6 + (512 * j) // 2048}" for j in range(NJ)]

            def front_apply(blk):
                xb_apply(blk, True, f"{tag}b{blk}", arX[:, 0:4160],
                         [f"X{j}" for j in range(NJ)])
                A("gpsimd", lambda v: v.tensor_tensor(
                    arS[:, 0:4096].rearrange("p (j c) -> p j c", c=512),
                    arX[:, 0:4160].rearrange("p (j c) -> p j c", c=520)[:, :, 1:513],
                    arX[:, 0:4160].rearrange("p (j c) -> p j c", c=520)[:, :, 0:512],
                    op=ALU.subtract),
                  reads=[f"X{j}" for j in range(NJ)], writes=[SK[0], SK[1], SK[2], SK[3]],
                  tag=f"{tag}df{blk}")

            def front_mix(blk):
                xr_v, xr_k, xk_v, xk_k = bufs(blk)
                for j in range(NJ):
                    dv = arS[:, 512 * j:512 * (j + 1)]
                    dk = SK[(512 * j) // 1024]
                    A("vector", lambda v, j=j, dv=dv, xr=xr_v[j]: v.scalar_tensor_tensor(
                        xr, dv, muT["r"][:, j:j + 1], arX[:, 520 * j:520 * j + 512],
                        op0=ALU.mult, op1=ALU.add),
                      reads=[dk, "mur", f"X{j}"], writes=[xr_k[j]], tag=f"{tag}xr{blk}_{j}")
                    A("vector", lambda v, j=j, dv=dv, xk=xk_v[j]: v.scalar_tensor_tensor(
                        xk, dv, muT["k"][:, j:j + 1], arX[:, 520 * j:520 * j + 512],
                        op0=ALU.mult, op1=ALU.add),
                      reads=[dk, "muk", f"X{j}"], writes=[xk_k[j]], tag=f"{tag}xk{blk}_{j}")

            def back_r(blk):
                xr_v, xr_k, xk_v, xk_k = bufs(blk)
                for m in range(NJ):
                    ps, pk = proj_mm(0, xr_v, xr_k, m, m % 3, f"{tag}r{blk}_{m}")
                    A("scalar", lambda s, m=m, ps=ps: s.activation(
                        arS[:, 12288 + 512 * m:12288 + 512 * (m + 1)], ps[:, 0:512],
                        AF.Tanh, scale=0.5),
                      reads=[pk], writes=[SK[(12288 + 512 * m) // 1024]], tag=f"{tag}sg{blk}_{m}")
                    A("scalar", lambda s, m=m: s.activation(
                        arS[:, 12288 + 512 * m:12288 + 512 * (m + 1)],
                        arS[:, 12288 + 512 * m:12288 + 512 * (m + 1)],
                        AF.Identity, scale=0.5, bias=halfc[:, 0:1]),
                      reads=[SK[(12288 + 512 * m) // 1024], "halfc"],
                      writes=[SK[(12288 + 512 * m) // 1024]], tag=f"{tag}sa{blk}_{m}")

            def back_k(blk):
                xr_v, xr_k, xk_v, xk_k = bufs(blk)
                for m in range(NJ):
                    ps, pk = proj_mm(8, xk_v, xk_k, m, m % 3, f"{tag}k{blk}_{m}")
                    kv = arS[:, 16384 + 512 * m:16384 + 512 * (m + 1)]
                    kk = SK[(16384 + 512 * m) // 1024]
                    A("scalar", lambda s, ps=ps, kv=kv: s.activation(kv, ps[:, 0:512], AF.Relu),
                      reads=[pk], writes=[kk], tag=f"{tag}rl{blk}_{m}")
                    A("gpsimd", lambda v, kv=kv: v.tensor_tensor(kv, kv, kv, op=ALU.mult),
                      reads=[kk], writes=[kk], tag=f"{tag}sq{blk}_{m}")

            def back_v(blk):
                ka_v = [arS[:, 16384 + 512 * j:16384 + 512 * (j + 1)] for j in range(NJ)]
                ka_k = [SK[(16384 + 512 * j) // 1024] for j in range(NJ)]
                for m in range(NJ):
                    ps, pk = proj_mm(16, ka_v, ka_k, m, m % 3, f"{tag}v{blk}_{m}")
                    A("vector", lambda v, m=m, ps=ps: v.scalar_tensor_tensor(
                        out_v[m], ps[:, 0:512], halfc[:, 0:1],
                        arS[:, 12288 + 512 * m:12288 + 512 * (m + 1)],
                        op0=ALU.mult, op1=ALU.mult),
                      reads=[pk, SK[(12288 + 512 * m) // 1024], "halfc"],
                      writes=[out_k[m]], tag=f"{tag}o{blk}_{m}")
                for u in range(4):
                    ti = 4 * blk + u
                    pb = 1024 * (u % 2)
                    pbk = f"pD{u % 2}"
                    i0 = None
                    for m in range(NJ):
                        i0 = A("tensor", lambda t, m=m, u=u, pb=pb: t.transpose(
                            pDb[:, pb + 128 * m:pb + 128 * (m + 1)],
                            out_v[m][:, 128 * u:128 * (u + 1)], idb[:, :]),
                            reads=[out_k[m], "idb"],
                            writes=([pbk] if m == 0 else []), tag=f"{tag}ot{blk}_{u}_{m}")
                    pr.set_last_writer(pbk, i0)
                    A("vector", lambda v, ti=ti, pb=pb: v.tensor_tensor(
                        resid[ti][:, :], resid[ti][:, :], pDb[:, pb:pb + 1024], op=ALU.add),
                      reads=[pbk, f"res{ti}"], writes=[f"res{ti}"], tag=f"{tag}ra{blk}_{u}")
                    if tag == "f1":
                        ln_stats(ti, u % 2, f"qpre{ti}")

            phase_stats(f"{tag}ps", tiles=not pre_stats)
            front_apply(0)
            front_mix(0)
            for blk in range(NBLK):
                back_r(blk)
                if blk + 1 < NBLK:
                    front_apply(blk + 1)
                back_k(blk)
                if blk + 1 < NBLK:
                    front_mix(blk + 1)
                back_v(blk)

        ffn_phase("f1")

        for i in range(24):
            A("sync", lambda e, i=i: e.dma_start(
                out=wbf[:, 1024 * i:1024 * (i + 1)], in_=wslot(i)[:, :]),
              reads=[WK[i]], writes=["wbf"], kind="dma", tag=f"wsv{i}")

        for wi, w in enumerate(("q", "k", "v")):
            load_weight_chunks(ct[w], 8 * wi, f"qkv_{w}")
        vt_bufs = [(arS[:, 8192:9232], [SK[8], SK[9]]), (arS[:, 14400:15440], [SK[14], SK[15]])]
        A("vector", lambda v: v.memset(arS[:, 8192:9232], 1.0),
          writes=[SK[8], SK[9]], tag="vtones0")
        A("vector", lambda v: v.memset(arS[:, 14400:15440], 1.0),
          writes=[SK[14], SK[15]], tag="vtones1")
        A("vector", lambda v: v.memset(mstats[:, 0:32], 0.0), writes=["mstats"], tag="macc0")

        def qkv_xv(blk):
            if blk % 2 == 0:
                return arX[:, 0:4160], [f"X{j}" for j in range(NJ)]
            return arS[:, 10240:14400], [SK[10 + i] for i in range(5)]

        def qkv_q(blk):
            xv, xkeys = qkv_xv(blk)
            xln_views = [xv[:, 520 * j + 1:520 * j + 513] for j in range(NJ)]
            xk2 = [xkeys[(520 * j) // 1024] for j in range(NJ)] if len(xkeys) == 5 else xkeys
            for m in range(NJ):
                ps, pk = proj_mm(0, xln_views, xk2, m, m % 3, f"qq{blk}_{m}")
                A("scalar", lambda s, m=m, blk=blk, ps=ps: s.activation(
                    arQ[:, 2048 * m + 512 * blk:2048 * m + 512 * (blk + 1)], ps[:, 0:512],
                    AF.Identity, bias=bT["q"][:, m:m + 1]),
                  reads=[pk, "bq"], writes=[f"Q{m}"], tag=f"qe{blk}_{m}")
                A("scalar", lambda s, m=m, blk=blk: s.activation(
                    sqsc[:, :], arQ[:, 2048 * m + 512 * blk:2048 * m + 512 * (blk + 1)],
                    AF.Square),
                  reads=[f"Q{m}"], writes=["sqsc"], tag=f"qsq{blk}_{m}")
                pdq = 512 * (m % 2)
                pdk_ = f"pD{m % 2}"
                A("tensor", lambda t, pdq=pdq: t.matmul(
                    pD[0:2, pdq:pdq + 512], maskq[:, :], sqsc[:, :], start=True, stop=True),
                    reads=["maskq", "sqsc"], writes=[pdk_], tag=f"qsm{blk}_{m}")
                A("vector", lambda v, pdq=pdq: v.reduce_max(
                    mstats[0:2, 112:113], pD[0:2, pdq:pdq + 512], axis=AX.X),
                  reads=[pdk_], writes=["mred"], tag=f"qsr{blk}_{m}")
                A("vector", lambda v, m=m: v.tensor_tensor(
                    mstats[0:2, m:m + 1], mstats[0:2, m:m + 1], mstats[0:2, 112:113],
                    op=ALU.max),
                  reads=["mred", "mstats"], writes=["mstats"], tag=f"qsx{blk}_{m}")

        def qkv_k(blk):
            xv, xkeys = qkv_xv(blk)
            xln_views = [xv[:, 520 * j + 1:520 * j + 513] for j in range(NJ)]
            xk2 = [xkeys[(520 * j) // 1024] for j in range(NJ)] if len(xkeys) == 5 else xkeys
            for m in range(NJ):
                ps, pk = proj_mm(8, xln_views, xk2, m, m % 3, f"qk{blk}_{m}")
                A("scalar", lambda s, m=m, ps=ps: s.activation(
                    arS[:, 512 * m:512 * (m + 1)], ps[:, 0:512],
                    AF.Identity, bias=bT["k"][:, m:m + 1]),
                  reads=[pk, "bk"], writes=[SK[(512 * m) // 1024]], tag=f"qke{blk}_{m}")
                A("sync", lambda g, m=m, blk=blk: g.dma_start(
                    out=kC[128 * m:128 * (m + 1), 512 * blk:512 * (blk + 1)],
                    in_=arS[:, 512 * m:512 * (m + 1)]),
                  reads=[SK[(512 * m) // 1024]], writes=[f"kC{m}"], kind="dma", tag=f"qkd{blk}_{m}")
                A("scalar", lambda s, m=m: s.activation(
                    sqsc[:, :], arS[:, 512 * m:512 * (m + 1)], AF.Square),
                  reads=[SK[(512 * m) // 1024]], writes=["sqsc"], tag=f"ksq{blk}_{m}")
                pdq = 512 * (m % 2)
                pdk_ = f"pD{m % 2}"
                A("tensor", lambda t, pdq=pdq: t.matmul(
                    pD[0:2, pdq:pdq + 512], maskq[:, :], sqsc[:, :], start=True, stop=True),
                    reads=["maskq", "sqsc"], writes=[pdk_], tag=f"ksm{blk}_{m}")
                A("vector", lambda v, pdq=pdq: v.reduce_max(
                    mstats[0:2, 113:114], pD[0:2, pdq:pdq + 512], axis=AX.X),
                  reads=[pdk_], writes=["mredk"], tag=f"ksr{blk}_{m}")
                A("vector", lambda v, m=m: v.tensor_tensor(
                    mstats[0:2, 16 + m:16 + m + 1], mstats[0:2, 16 + m:16 + m + 1],
                    mstats[0:2, 113:114], op=ALU.max),
                  reads=["mredk", "mstats"], writes=["mstats"], tag=f"ksx{blk}_{m}")

        def qkv_v(blk):
            xv, xkeys = qkv_xv(blk)
            xln_views = [xv[:, 520 * j + 1:520 * j + 513] for j in range(NJ)]
            xk2 = [xkeys[(520 * j) // 1024] for j in range(NJ)] if len(xkeys) == 5 else xkeys
            for m in range(NJ):
                ps, pk = proj_mm(16, xln_views, xk2, m, m % 3, f"qv{blk}_{m}")
                A("scalar", lambda s, m=m, ps=ps: s.activation(
                    arS[:, 4096 + 512 * m:4096 + 512 * (m + 1)], ps[:, 0:512],
                    AF.Identity, bias=bT["v"][:, m:m + 1]),
                  reads=[pk, "bv"], writes=[SK[(4096 + 512 * m) // 1024]], tag=f"qve{blk}_{m}")
            for u in range(4):
                cglob = 4 * blk + u
                pb = 1024 * (u % 2)
                pbk = f"pD{u % 2}"
                i0 = None
                for m in range(NJ):
                    i0 = A("tensor", lambda t, m=m, u=u, pb=pb: t.transpose(
                        pDb[:, pb + 128 * m:pb + 128 * (m + 1)],
                        arS[:, 4096 + 512 * m + 128 * u:4096 + 512 * m + 128 * (u + 1)], idb[:, :]),
                        reads=[SK[(4096 + 512 * m) // 1024], "idb"],
                        writes=([pbk] if m == 0 else []), tag=f"vt{blk}_{u}_{m}")
                pr.set_last_writer(pbk, i0)
                vtv, vtk = vt_bufs[u % 2]
                A("vector", lambda v, pb=pb, vtv=vtv: v.tensor_copy(
                    vtv.rearrange("p (h m) -> p h m", m=65)[:, :, 0:64],
                    pDb[:, pb:pb + 1024].rearrange("p (h m) -> p h m", m=64)),
                  reads=[pbk], writes=vtk, tag=f"ve{blk}_{u}")
                for q0 in range(4):
                    A("sync", lambda g, q0=q0, cglob=cglob, vtv=vtv: g.dma_start(
                        out=vC[:, :].rearrange("(h p) m -> p h m", p=128)[:, 4 * q0:4 * (q0 + 1), 65 * cglob:65 * (cglob + 1)],
                        in_=vtv[:, 65 * 4 * q0:65 * 4 * (q0 + 1)].rearrange("p (h m) -> p h m", m=65)),
                      reads=vtk, writes=[f"vC{q0}"], kind="dma", tag=f"vd{blk}_{u}_{q0}")

        phase_stats("qps", tiles=False)
        xv0, xkeys0 = qkv_xv(0)
        xb_apply(0, False, "q0", xv0, xkeys0)
        for blk in range(NBLK):
            qkv_q(blk)
            if blk + 1 < NBLK:
                xvn, xkeysn = qkv_xv(blk + 1)
                xb_apply(blk + 1, False, f"q{blk + 1}", xvn, xkeysn)
            qkv_k(blk)
            qkv_v(blk)

        def cc_kv(hp):
            A("gpsimd", lambda g, hp=hp: g.collective_compute(
                "AllGather", ALU.bypass, replica_groups=PAIRS,
                ins=[kC[128 * hp:128 * (hp + 1), :].opt()],
                outs=[kG[256 * hp:256 * (hp + 1), :].opt()]),
              reads=[f"kC{hp}"], writes=[f"kG{hp}"], kind="cc", tag=f"ccK{hp}")
            A("gpsimd", lambda g, hp=hp: g.collective_compute(
                "AllGather", ALU.bypass, replica_groups=PAIRS,
                ins=[vC[256 * hp:256 * (hp + 1), :].opt()],
                outs=[vG[512 * hp:512 * (hp + 1), :].opt()]),
              reads=[f"vC{hp // 2}"], writes=[f"vG{hp}"], kind="cc", tag=f"ccV{hp}")
        A("gpsimd", lambda g, hp=0: g.collective_compute(
            "AllGather", ALU.bypass, replica_groups=PAIRS,
            ins=[kC[0:128, :].opt()], outs=[kG[0:256, :].opt()]),
          reads=["kC0"], writes=["kG0"], kind="cc", tag="ccK0")
        A("sync", lambda g: g.dma_start(out=mxC[:, :], in_=mstats[0:2, 16:32]),
          reads=["mstats"], writes=["mxC"], kind="dma", tag="mxw")
        A("gpsimd", lambda g: g.collective_compute(
            "AllReduce", ALU.max, replica_groups=PAIRS,
            ins=[mxC[:, :].opt()], outs=[mxR[:, :].opt()]),
          reads=["mxC"], writes=["mxR"], kind="cc", tag="ccM")
        A("sync", lambda e: e.dma_start(out=mstats[0:2, 16:32], in_=mxR[:, :]),
          reads=["mxR", "mstats"], writes=["mstats"], kind="dma", tag="mxr")
        A("gpsimd", lambda g: g.collective_compute(
            "AllGather", ALU.bypass, replica_groups=PAIRS,
            ins=[vC[0:256, :].opt()], outs=[vG[0:512, :].opt()]),
          reads=["vC0"], writes=["vG0"], kind="cc", tag="ccV0")
        for hp in range(1, NJ):
            cc_kv(hp)
        A("vector", lambda v: v.tensor_tensor(mstats[0:2, 32:48], mstats[0:2, 0:16],
                                              mstats[0:2, 16:32], op=ALU.mult),
          reads=["mstats"], writes=["mstats"], tag="mprod")
        A("vector", lambda v: v.tensor_scalar(mstats[0:2, 32:48], mstats[0:2, 32:48],
                                              1e-30, None, op0=ALU.add),
          reads=["mstats"], writes=["mstats"], tag="mg")
        A("scalar", lambda s: s.activation(mstats[0:2, 96:112], mstats[0:2, 32:48], AF.Ln),
          reads=["mstats"], writes=["mstats"], tag="mln")
        A("scalar", lambda s: s.activation(mstats[0:2, 96:112], mstats[0:2, 96:112],
                                           AF.Exp, scale=0.5),
          reads=["mstats"], writes=["mstats"], tag="mex")
        A("vector", lambda v: v.tensor_scalar(mstats[0:2, 96:112], mstats[0:2, 96:112],
                                              0.125, None, op0=ALU.mult),
          reads=["mstats"], writes=["mstats"], tag="msc")
        A("gpsimd", lambda g: g.dma_start(
            out=mrow[0:1, 0:16].rearrange("p (m two) -> p m two", two=2)[:, :, 0:1],
            in_=mstats[0:1, 96:104], allow_slow_non_contiguous=True),
          reads=["mstats"], writes=["mrow"], kind="dma", tag="mrow0")
        A("gpsimd", lambda g: g.dma_start(
            out=mrow[0:1, 0:16].rearrange("p (m two) -> p m two", two=2)[:, :, 1:2],
            in_=mstats[1:2, 96:104], allow_slow_non_contiguous=True),
          reads=["mstats", "mrow"], writes=["mrow"], kind="dma", tag="mrow1")
        A("vector", lambda v: v.tensor_tensor(
            mrow[0:1, 16:24].rearrange("p (h one) -> p h one", one=1),
            mrow[0:1, 0:16].rearrange("p (h two) -> p h two", two=2)[:, :, 0:1],
            mrow[0:1, 0:16].rearrange("p (h two) -> p h two", two=2)[:, :, 1:2],
            op=ALU.max),
          reads=["mrow"], writes=["mrow"], tag="mpair")
        A("tensor", lambda t: t.matmul(pD[:, 0:8], negonesf[0:1, :], mrow[0:1, 16:24],
                                       start=True, stop=True),
          reads=["negonesf", "mrow"], writes=["pD0"], tag="mbc")
        A("vector", lambda v: v.tensor_copy(mne[:, :], pD[:, 0:8]),
          reads=["pD0"], writes=["mne"], tag="mne")

        vhs_off = [8192, 12352]
        av_banks = [(pC, "pCa", "pCb"), (pD, "pD0", "pD1")]
        att_pend = [None]

        def att_flush():
            if att_pend[0] is not None:
                avfn, epifn = att_pend[0]
                att_pend[0] = None
                avfn()
                if epifn is not None:
                    epifn()

        def make_epilogue(hp, s):
            pV, ka, kb = av_banks[s % 2]

            def epi():
                A("vector", lambda v, hp=hp, s=s, pV=pV: v.tensor_copy(
                    arS[0:64, 2048 * hp + 512 * s:2048 * hp + 512 * (s + 1)],
                    pV[0:64, 0:512]),
                  reads=[ka], writes=[SK[2 * hp], SK[2 * hp + 1]],
                  tag=f"ameA{hp}_{s}")
                A("vector", lambda v, hp=hp, s=s, pV=pV: v.tensor_copy(
                    arS[64:128, 2048 * hp + 512 * s:2048 * hp + 512 * (s + 1)],
                    pV[0:64, 512:1024]),
                  reads=[kb], writes=[SK[2 * hp], SK[2 * hp + 1]],
                  tag=f"ameB{hp}_{s}")
                A("vector", lambda v, pV=pV: v.tensor_copy(arWf[64:65, 10240:10752], pV[64:65, 0:512]),
                  reads=[ka], writes=wkeys(20480, 21504), tag=f"ad1{hp}_{s}")
                A("vector", lambda v, pV=pV: v.tensor_copy(arWf[64:65, 10752:11264], pV[64:65, 512:1024]),
                  reads=[kb, *wkeys(20480, 21504)], writes=wkeys(21504, 22528), tag=f"ad2{hp}_{s}")
                A("gpsimd", lambda g_, hp=hp, s=s: g_.dma_start(
                    out=arSf[2 * hp:2 * hp + 1, 8192 + 512 * s:8192 + 512 * (s + 1)],
                    in_=arWf[64:65, 10240:10752]),
                  reads=wkeys(20480, 21504), writes=[f"den{2 * hp}_{s}"],
                  kind="dma", tag=f"admA{hp}_{s}")
                A("gpsimd", lambda g_, hp=hp, s=s: g_.dma_start(
                    out=arSf[2 * hp + 1:2 * hp + 2, 8192 + 512 * s:8192 + 512 * (s + 1)],
                    in_=arWf[64:65, 10752:11264]),
                  reads=wkeys(21504, 22528), writes=[f"den{2 * hp + 1}_{s}"],
                  kind="dma", tag=f"admB{hp}_{s}")
            return epi

        for hp in range(NJ):
            p2 = hp % 2
            kc = 4096 * p2
            kkey = f"Khs{p2}"
            vc = vhs_off[p2]
            vkey = f"Vh{p2}"
            for r in range(2):
                A("sync", lambda e, hp=hp, r=r, kc=kc: e.dma_start(
                    out=arW[:, kc + 2048 * r:kc + 2048 * (r + 1)],
                    in_=kG[256 * hp + 128 * r:256 * hp + 128 * (r + 1), :]),
                  reads=[f"kG{hp}"], writes=[kkey] + wkeys(kc + 2048 * r, kc + 2048 * (r + 1)),
                  kind="dma", tag=f"akl{hp}_{r}")
                for tt in range(2):
                    A("sync", lambda e, hp=hp, r=r, tt=tt, vc=vc: e.dma_start(
                        out=arW[:, vc + 2080 * tt + 1040 * r:vc + 2080 * tt + 1040 * (r + 1)],
                        in_=vG[512 * hp + 256 * r + 128 * tt:512 * hp + 256 * r + 128 * (tt + 1), :]),
                      reads=[f"vG{hp}"],
                      writes=[vkey] + wkeys(vc + 2080 * tt + 1040 * r, vc + 2080 * tt + 1040 * (r + 1)),
                      kind="dma", tag=f"avl{hp}_{r}_{tt}")
            for s in range(4):
                pV, ka, kb = av_banks[s % 2]

                def emit_av(g, pt, ptk, hp=hp, s=s, vc=vc, vkey=vkey, pV=pV, ka=ka, kb=kb):
                    last = None
                    for tt in range(2):
                        vcol = vc + 2080 * tt + 1040 * (g // 16) + 65 * (g % 16)
                        last = A("tensor", lambda t, g=g, tt=tt, vcol=vcol, pt=pt, pV=pV: t.matmul(
                            pV[0:65, 512 * tt:512 * (tt + 1)],
                            arW[:, vcol:vcol + 65], pt[:, 512 * tt:512 * (tt + 1)],
                            start=(g == 0), stop=(g == 31)),
                            reads=[vkey, ptk] + ([ka, kb] if (g, tt) == (0, 0) else []),
                            writes=([ka, kb] if (g, tt) == (0, 0) else []),
                            tag=f"av{hp}_{s}_{g}_{tt}")
                    if g == 31:
                        pr.set_last_writer(ka, last)
                        pr.set_last_writer(kb, last)

                for g in range(32):
                    pS = pA if g % 2 == 0 else pB
                    pSk = "pA" if g % 2 == 0 else "pB"
                    pt = arW[:, 18432 + 1024 * (g % 2):18432 + 1024 * (g % 2 + 1)]
                    ptk = WK[18 + g % 2]
                    last = None
                    for tt in range(2):
                        rb = 64 * tt
                        last = A("tensor", lambda t, g=g, tt=tt, s=s, pS=pS, rb=rb, hp=hp, kc=kc: t.matmul(
                            pS[:, 512 * tt:512 * (tt + 1)],
                            arW[rb:rb + 64, kc + 128 * g:kc + 128 * (g + 1)],
                            arQ[rb:rb + 64, 2048 * hp + 512 * s:2048 * hp + 512 * (s + 1)],
                            start=True, stop=True),
                            reads=[kkey, f"Q{hp}"], writes=([pSk] if tt == 0 else []),
                            tag=f"as{hp}_{s}_{g}_{tt}")
                    pr.set_last_writer(pSk, last)
                    A("scalar", lambda sc, pS=pS, pt=pt, hp=hp: sc.activation(
                        pt, pS[:, :], AF.Exp, bias=mne[:, hp:hp + 1], scale=0.125),
                      reads=[pSk, "mne"], writes=[ptk], tag=f"ae{hp}_{s}_{g}")
                    att_flush()
                    att_pend[0] = ((lambda g=g, pt=pt, ptk=ptk, f=emit_av: f(g, pt, ptk)),
                                   (make_epilogue(hp, s) if g == 31 else None))
        att_flush()

        wo_views = [arQ[:, 8192 + 1024 * j:8192 + 1024 * (j + 1)] for j in range(6)] + \
                   [arX[:, 0:1024], arX[:, 1024:2048]]
        wo_keys = [[f"Q{4 + j // 2}"] for j in range(6)] + \
                  [[f"X{j}" for j in range(NJ)]] * 2
        for j in range(NJ):
            stg = arQf[:, 2048 + 1024 * (j % 2):2048 + 1024 * (j % 2 + 1)]
            sks = [f"Q{2 + j % 2}"]
            A("sync", lambda e, j=j, stg=stg: e.dma_start(
                out=stg, in_=ct["o"][128 * j:128 * (j + 1), :]),
              writes=sks, kind="dma", tag=f"wold{j}")
            A("vector", lambda v, j=j, stg=stg: v.tensor_copy(wo_views[j][:, :], stg),
              reads=sks, writes=wo_keys[j], tag=f"wocv{j}")

        A("vector", lambda v: v.tensor_copy(oddm[:, :], oddm[:, :]),
          writes=WK + ["Khs0", "Khs1", "Vh0", "Vh1"], reads=["oddm"], tag="fence2")

        den_keys = [f"den{h}_{s}" for h in range(NH) for s in range(4)]
        A("vector", lambda v: v.reciprocal(arSf[0:16, 8192:10240], arSf[0:16, 8192:10240]),
          reads=den_keys, writes=["rden"], tag="recip")
        for j in range(NJ):
            bb = 2048 * (j % 2)
            bk = ["Q0", "Q1"] if j % 2 == 0 else ["Q2", "Q3"]
            A("gpsimd", lambda g, j=j, bb=bb: g.dma_start(
                out=arQf[0:2, bb:bb + 2048],
                in_=arSf[2 * j:2 * (j + 1), 8192:10240]),
              reads=["rden"], writes=bk, kind="dma", tag=f"rc{j}")
            for half in range(2):
                for u in range(2):
                    A("tensor", lambda t, u=u, half=half, bb=bb: t.matmul(
                        pD[:, 512 * u:512 * (u + 1)], pairmask[:, :],
                        arQf[0:2, bb + 1024 * half + 512 * u:bb + 1024 * half + 512 * (u + 1)],
                        start=True, stop=True),
                        reads=["pairmask"] + bk,
                        writes=[f"pD{u}"], tag=f"rb{j}_{half}_{u}")
                A("vector", lambda v, j=j, half=half: v.tensor_tensor(
                    arS[:, 2048 * j + 1024 * half:2048 * j + 1024 * (half + 1)],
                    arS[:, 2048 * j + 1024 * half:2048 * j + 1024 * (half + 1)],
                    pD[:, 0:1024], op=ALU.mult),
                  reads=["pD0", "pD1", SK[2 * j], SK[2 * j + 1]],
                  writes=[SK[2 * j], SK[2 * j + 1]], tag=f"rs{j}_{half}")

        merge_views = [arS[:, 2048 * j:2048 * (j + 1)] for j in range(NJ)]
        for blk in (3, 0, 1, 2):
            mv = [merge_views[j][:, 512 * blk:512 * (blk + 1)] for j in range(NJ)]
            mk = [SK[(2048 * j + 512 * blk) // 1024] for j in range(NJ)]
            for m in range(NJ):
                ps, pk = proj_mm(0, mv, mk, m, m % 3, f"wo{blk}_{m}",
                                 w_views=wo_views, w_keys=wo_keys)
                A("scalar", lambda s, m=m, ps=ps: s.activation(
                    arS[:, 16384 + 512 * m:16384 + 512 * (m + 1)], ps[:, 0:512],
                    AF.Identity, bias=bT["o"][:, m:m + 1]),
                  reads=[pk, "bo"],
                  writes=[SK[(16384 + 512 * m) // 1024], "rden"],
                  tag=f"woe{blk}_{m}")
            for u in range(4):
                ti = 4 * blk + u
                pb = 1024 * (u % 2)
                pbk = f"pD{u % 2}"
                i0 = None
                for m in range(NJ):
                    i0 = A("tensor", lambda t, m=m, u=u, pb=pb: t.transpose(
                        pDb[:, pb + 128 * m:pb + 128 * (m + 1)],
                        arS[:, 16384 + 512 * m + 128 * u:16384 + 512 * m + 128 * (u + 1)], idb[:, :]),
                        reads=[SK[(16384 + 512 * m) // 1024], "idb"],
                        writes=([pbk] if m == 0 else []), tag=f"wot{blk}_{u}_{m}")
                pr.set_last_writer(pbk, i0)
                A("vector", lambda v, ti=ti, pb=pb: v.tensor_tensor(
                    resid[ti][:, :], resid[ti][:, :], pDb[:, pb:pb + 1024], op=ALU.add),
                  reads=[pbk, f"res{ti}"], writes=[f"res{ti}"], tag=f"woa{blk}_{u}")
                ln_stats(ti, u % 2, f"f2pre{ti}")
            if blk == 3:
                A("gpsimd", lambda g: g.dma_start(out=bC[:, :], in_=resid[15][127:128, :]),
                  reads=["res15"], writes=["bC"], kind="dma", tag="bcw")
                A("gpsimd", lambda g: g.collective_compute(
                    "AllGather", ALU.bypass, replica_groups=PAIRS,
                    ins=[bC[:, :].opt()], outs=[bG[:, :].opt()]),
                  reads=["bC"], writes=["bG"], kind="cc", tag="ccB")
                A("sync", lambda e: e.dma_start(out=arWf[0:1, 4096:5120], in_=bG[0:1, :]),
                  reads=["bG"], writes=wkeys(8192, 10240), kind="dma", tag="bgr")

        A("vector", lambda v: v.tensor_scalar(resb, arWf[0:1, 4096:5120],
                                              oddm[0:1, 0:1], None, op0=ALU.mult),
          reads=[*wkeys(8192, 10240), "oddm"], writes=["resb", SK[14], SK[15]], tag="bmask")

        ffn_phase("f2", pre_stats=True, from_cache=True)

        for i in range(NTL):
            A("sync", lambda e, i=i: e.dma_start(out=y_out[128 * i:128 * (i + 1), :], in_=resid[i][:, :]),
              reads=[f"res{i}"], kind="dma", tag=f"st{i}")

        pr.emit(ex)
    return nc


_NC_CACHE = []


def kernel(L0_attn_x, L0_attn_wq_c, L0_attn_wq_b, L0_attn_wk_c, L0_attn_wk_b,
           L0_attn_wv_c, L0_attn_wv_b, L0_attn_wo_c, L0_attn_wo_b,
           L0_ffn_W_r, L0_ffn_W_k, L0_ffn_W_v, L0_ffn_mu_r, L0_ffn_mu_k):
    x = np.asarray(L0_attn_x, dtype=np.float32)
    B, TT, DD = x.shape
    assert (B, TT, DD) == (4, 4096, 1024)

    idx = (np.arange(D)[None, :] - np.arange(D)[:, None]) % D
    def circ(wc):
        return np.ascontiguousarray(np.asarray(wc, np.float32)[idx])
    def fm8(v):
        return np.ascontiguousarray(np.asarray(v, np.float32).reshape(NJ, 128).T)

    common = {
        "idf": np.eye(128, dtype=np.float32),
        "ct_q": circ(L0_attn_wq_c), "ct_k": circ(L0_attn_wk_c),
        "ct_v": circ(L0_attn_wv_c), "ct_o": circ(L0_attn_wo_c),
        "b_q": fm8(L0_attn_wq_b), "b_k": fm8(L0_attn_wk_b),
        "b_v": fm8(L0_attn_wv_b), "b_o": fm8(L0_attn_wo_b),
        "mu_r": fm8(L0_ffn_mu_r), "mu_k": fm8(L0_ffn_mu_k),
        "wrT": np.ascontiguousarray(np.asarray(L0_ffn_W_r, np.float32).T),
        "wkT": np.ascontiguousarray(np.asarray(L0_ffn_W_k, np.float32).T),
        "wvT": np.ascontiguousarray(np.asarray(L0_ffn_W_v, np.float32).T),
    }
    zrow = np.zeros((1, D), np.float32)
    in_maps = []
    for c in range(8):
        seq, half = c // 2, c % 2
        t0 = half * T
        m = dict(common)
        m["x"] = np.ascontiguousarray(x[seq, t0:t0 + T, :])
        m["xbnd"] = np.ascontiguousarray(x[seq, t0 - 1:t0, :]) if half else zrow
        m["oddmask"] = np.array([[float(half)]], np.float32)
        in_maps.append(m)

    if not _NC_CACHE:
        _NC_CACHE.append(build_nc())
    nc = _NC_CACHE[0]
    res = run_bass_kernel_spmd(nc, in_maps, core_ids=list(range(8)))
    y = np.empty((4, 4096, 1024), np.float32)
    for c in range(8):
        seq, half = c // 2, c % 2
        y[seq, half * T:(half + 1) * T, :] = np.asarray(res.results[c]["y"])
    return y
